# version 02
# baseline (speedup 1.0000x reference)
import math
import numpy as np
import ml_dtypes
import concourse.bass as bass
import concourse.mybir as mybir
from concourse.bass_utils import run_bass_kernel_spmd

F32 = mybir.dt.float32
BF16 = mybir.dt.bfloat16
I32 = mybir.dt.int32
AF = mybir.ActivationFunctionType
ALU = mybir.AluOpType
AX = mybir.AxisListType
NPBF = ml_dtypes.bfloat16


def ap_base(ap):
    b = ap.base_partition
    return b() if callable(b) else b


def ap_parts(ap):
    s = ap.shape
    s = s() if callable(s) else s
    return s[0]

T = 4096
D = 1024
NQC = 8
QC = 512
DBG_L1A = False
NCH = D // 128
ENGINES = ("tensor", "vector", "scalar", "gpsimd", "sync")
LN_EPS = 1e-5



class Sem:
    __slots__ = ("name", "n", "h")

    def __init__(self, name):
        self.name, self.n, self.h = name, 0, None


class Arena:
    def __init__(self, handle, cols, name):
        self.h, self.cols, self.off, self.name = handle, cols, 0, name
        self.marks = []

    def alloc(self, cols, parts=128, p0=0):
        cols = int(cols)
        assert self.off + cols <= self.cols, \
            f"arena {self.name}: {self.off}+{cols}>{self.cols}"
        ap = self.h[p0:p0 + parts, self.off:self.off + cols]
        self.off += cols
        return ap

    def push(self):
        self.marks.append(self.off)

    def pop(self):
        self.off = self.marks.pop()

    def row_alloc(self, cols, parts=1, p0=True):
        off = self.off
        self.alloc(cols)
        return self.h[0:parts, off:off + cols]


class Prog:

    def __init__(self, nc):
        self.nc = nc
        self.q = {e: [] for e in ENGINES}
        self.sems = {}
        self.waited = {e: {} for e in ENGINES}
        self.dirty = {e: False for e in ENGINES}
        self.last_ins = {}
        self.esem = {"tensor": "mm", "vector": "ve", "scalar": "ac",
                     "gpsimd": "gp", "sync": "sy"}
        for s in ("mm", "ve", "ac", "gp", "sy", "dg", "ds"):
            self.sem(s)
        self.dram = {}
        self.const_arrays = {}
        self._constkeys = {}
        self._cn = 0
        self.pfx = ""

    def sem(self, name):
        if name not in self.sems:
            self.sems[name] = Sem(name)
        return self.sems[name]

    def n(self, name):
        return self.sem(name).n

    def wait(self, eng, semname, val):
        if val is None or val <= 0:
            return
        w = self.waited[eng]
        if w.get(semname, 0) >= val:
            return
        w[semname] = val
        s = self.sem(semname)
        self.q[eng].append({"w": (s, val)})

    def ins(self, eng, build, waits=(), inc=None):
        for sn, v in waits:
            self.wait(eng, sn, v)
        self.dirty[eng] = True
        ent = {"b": build, "inc": None, "v": None}
        if inc is not None:
            sn, k = inc
            s = self.sem(sn)
            s.n += k
            ent["inc"] = (s, k)
            ent["v"] = (sn, s.n)
        self.q[eng].append(ent)
        self.last_ins[eng] = ent
        if inc is not None:
            return ent["v"][1]
        return None

    def barrier(self):
        markers = []
        for eng in ENGINES:
            if not self.dirty[eng]:
                continue
            ent = self.last_ins.get(eng)
            assert ent is not None
            if ent["inc"] is None:
                s = self.sem(self.esem[eng])
                s.n += 1
                ent["inc"] = (s, 1)
                ent["v"] = (self.esem[eng], s.n)
            markers.append(ent["v"])
        for sn in ("dg", "ds"):
            if self.sems[sn].n > 0:
                markers.append((sn, self.sems[sn].n))
        for eng in ENGINES:
            for sn, v in markers:
                self.wait(eng, sn, v)
            self.dirty[eng] = False

    def dram_t(self, name, shape, dtype, kind="Internal"):
        if name.startswith("s_"):
            name = self.pfx + name
        if name in self.dram:
            return self.dram[name]
        ap = self.nc.dram_tensor(name, list(shape), dtype, kind=kind).ap()
        self.dram[name] = ap
        return ap

    def const(self, arr, tag="c"):
        arr = np.ascontiguousarray(arr)
        key = (tag, arr.shape, str(arr.dtype), hash(arr.tobytes()))
        if key in self._constkeys:
            return self._constkeys[key]
        name = f"cst{self._cn}_{tag}"
        self._cn += 1
        dt = {np.dtype(np.float32): F32, np.dtype(np.int32): I32}[arr.dtype]
        ap = self.dram_t(name, arr.shape, dt, kind="ExternalInput")
        self.const_arrays[name] = arr
        self._constkeys[key] = ap
        return ap

    def gdma(self, out_ap, in_ap, waits=()):
        return self.ins("gpsimd", lambda e, o=out_ap, i=in_ap:
                        e.dma_start(out=o, in_=i), waits=waits, inc=("dg", 16))

    def gdma_acc(self, out_ap, in_ap, waits=()):
        return self.ins("gpsimd", lambda e, o=out_ap, i=in_ap:
                        e.dma_start(out=o, in_=i, accum_op=ALU.add),
                        waits=waits, inc=("dg", 16))

    def sdma(self, out_ap, in_ap, waits=()):
        return self.ins("sync", lambda e, o=out_ap, i=in_ap:
                        e.dma_start(out=o, in_=i), waits=waits, inc=("ds", 16))

    def mm(self, out, lhsT, rhs, start, stop, waits=(), inc=None):
        return self.ins("tensor", lambda e, o=out, l=lhsT, r=rhs, a=start, b=stop:
                        e.matmul(o, l, r, start=a, stop=b), waits=waits, inc=inc)

    def trans(self, out, in_, ident, waits=(), inc=None):
        return self.ins("tensor", lambda e, o=out, i=in_, d=ident:
                        e.transpose(o, i, d), waits=waits, inc=inc)

    def act(self, eng_out, in_, func, bias=None, scale=None, waits=(), inc=None):
        kw = {}
        if bias is not None:
            if isinstance(bias, (int, float)):
                bias, vv = self._R.fc(self, float(bias), ap_base(eng_out),
                                      ap_parts(eng_out))
                waits = list(waits) + [("ve", vv)]
            kw["bias"] = bias
        if scale is not None:
            kw["scale"] = scale
        return self.ins("scalar", lambda e, o=eng_out, i=in_, f=func, kw=kw:
                        e.activation(out=o, in_=i, func=f, **kw), waits=waits, inc=inc)

    def scopy(self, out, in_, waits=(), inc=None):
        return self.ins("scalar", lambda e, o=out, i=in_: e.copy(o, i),
                        waits=waits, inc=inc)

    def tt(self, out, a, b, op, waits=(), inc=None):
        return self.ins("vector", lambda e, o=out, x=a, y=b, p=op:
                        e.tensor_tensor(out=o, in0=x, in1=y, op=p), waits=waits, inc=inc)

    def ts(self, out, in0, s1, op0, s2=None, op1=None, waits=(), inc=None):
        def b(e, o=out, i=in0, s1=s1, s2=s2, op0=op0, op1=op1):
            if op1 is None:
                return e.tensor_scalar(out=o, in0=i, scalar1=s1, scalar2=None,
                                       op0=op0, op1=ALU.bypass)
            return e.tensor_scalar(out=o, in0=i, scalar1=s1, scalar2=s2,
                                   op0=op0, op1=op1)
        return self.ins("vector", b, waits=waits, inc=inc)

    def stt(self, out, in0, scal, op0, in1, op1, waits=(), inc=None):
        return self.ins("vector", lambda e, o=out, a=in0, s=scal, b=in1, p0=op0, p1=op1:
                        e.scalar_tensor_tensor(out=o, in0=a, scalar=s, in1=b,
                                               op0=p0, op1=p1), waits=waits, inc=inc)

    def vcopy(self, out, in_, waits=(), inc=None):
        return self.ins("vector", lambda e, o=out, i=in_: e.tensor_copy(o, i),
                        waits=waits, inc=inc)

    def vred(self, out, in_, op, axis=AX.X, waits=(), inc=None):
        self.ins("vector", lambda e, o=out, i=in_, p=op, a=axis:
                 e.tensor_reduce(out=o, in_=i, axis=a, op=p), waits=waits)
        return self.ins("vector", lambda e: e.drain(), inc=inc)

    def vrecip(self, out, in_, waits=(), inc=None):
        self.ins("vector", lambda e, o=out, i=in_: e.reciprocal(o, i), waits=waits)
        return self.ins("vector", lambda e: e.drain(), inc=inc)

    def vmemset(self, ap, val, waits=(), inc=None):
        return self.ins("vector", lambda e, a=ap, v=val: e.memset(a, v),
                        waits=waits, inc=inc)

    def vscan(self, out, d0, d1, initial, op0, op1, waits=(), inc=None):
        r = self.ins("vector", lambda e, o=out, a=d0, b=d1, i=initial, p0=op0, p1=op1:
                     e.tensor_tensor_scan(out=o, data0=a, data1=b, initial=i,
                                          op0=p0, op1=p1), waits=waits)
        self.ins("vector", lambda e: e.drain())
        if inc is not None:
            return self.ins("vector", lambda e: e.nop(), inc=inc)
        return r

    def build(self):
        from contextlib import ExitStack
        es = ExitStack()
        E = es.enter_context
        nc = self.nc
        for s in self.sems.values():
            s.h = E(nc.semaphore(s.name))
        block = E(nc.Block())
        for eng_name, deco in (("gpsimd", block.gpsimd), ("sync", block.sync),
                               ("tensor", block.tensor), ("vector", block.vector),
                               ("scalar", block.scalar)):
            ents = self.q[eng_name]
            def body(e, ents=ents):
                for ent in ents:
                    if "w" in ent:
                        s, v = ent["w"]
                        e.wait_ge(s.h, v)
                    else:
                        i = ent["b"](e)
                        if ent["inc"] is not None:
                            s, k = ent["inc"]
                            i.then_inc(s.h, k)
            deco(body)
        es.close()
        return nc


class Res:

    def __init__(self, P, E):
        nc = P.nc
        self.pb = [E(nc.psum_tensor(f"pb{i}", [128, 512], F32)) for i in range(6)]
        self.pt = [E(nc.psum_tensor(f"pt{i}", [128, 1024], BF16)) for i in range(2)]
        self.wb = Arena(E(nc.sbuf_tensor("wb", [128, 26624], BF16)), 26624, "wb")
        self.xb = Arena(E(nc.sbuf_tensor("xb", [128, 26624], BF16)), 26624, "xb")
        self.fb = Arena(E(nc.sbuf_tensor("fb", [128, 7168], F32)), 7168, "fb")
        self.rw = Arena(E(nc.sbuf_tensor("rw", [128, 10240], F32)), 10240, "rw")
        self.cb = Arena(E(nc.sbuf_tensor("cb", [128, 6144], BF16)), 6144, "cb")
        self.ib = Arena(E(nc.sbuf_tensor("ib", [128, 512], I32)), 512, "ib")
        self.onesf = self.rw.alloc(1)
        P.vmemset(self.onesf, 1.0)
        self.onesrow_f = self.rw.alloc(128)[0:1, :]
        P.vmemset(self.onesrow_f, 1.0)
        self.onesrow_b = self.cb.alloc(128)[0:1, :]
        P.vmemset(self.onesrow_b, 1.0)
        self.onesb = self.cb.alloc(1)
        P.vmemset(self.onesb, 1.0, inc=("ve", 1))
        self.identb = self.cb.alloc(128)
        P.gdma(self.identb, P.const(np.eye(128, dtype=np.float32), "id")[:, :])
        self.identf = self.rw.alloc(128)
        P.gdma(self.identf, P.const(np.eye(128, dtype=np.float32), "id")[:, :])
        self._fc = {}
        self._fc_arena = Arena(E(nc.sbuf_tensor("fcs", [128, 64], F32)), 64, "fcs")
        P._R = self
        self.bankfree = {}
        P.barrier()

    def fconst(self, P, val):
        if val not in self._fc:
            a = self._fc_arena
            col = a.off
            t = a.alloc(1)
            vv = P.vmemset(t, float(val), inc=("ve", 1))
            self._fc[val] = (a.h, col, vv)
        return self._fc[val]

    def fc(self, P, val, base, parts):
        h, col, vv = self.fconst(P, val)
        return h[base:base + parts, col:col + 1], vv

    def bank_wait(self, bank):
        return self.bankfree.get(id(bank), None)

    def bank_set(self, bank, semname, val):
        self.bankfree[id(bank)] = (semname, val)


def bank_waits(R, bank):
    bw = R.bank_wait(bank)
    return [bw] if bw else []




def ln_pass(P, R, x_dram, h_dram, gamma_beta=None, double_ln=False, var_dram=None, h1_dram=None, rms_extra=False, mult_dram=None):
    gb = None
    if gamma_beta is not None:
        g_dr, b_dr = gamma_beta
        gcol = R.rw.alloc(NCH)
        bcol = R.rw.alloc(NCH)
        for c in range(NCH):
            P.gdma(gcol[0:128, c:c + 1], g_dr[c * 128:(c + 1) * 128])
            P.gdma(bcol[0:128, c:c + 1], b_dr[c * 128:(c + 1) * 128])
        gb = (gcol, bcol)

    R.fb.push()
    R.xb.push()
    xt = [R.fb.alloc(QC) for _ in range(NCH)]
    sq = [R.fb.alloc(QC), R.fb.alloc(QC)]
    tmp = R.fb.alloc(QC)
    R.rw.push()
    r_row = R.rw.row_alloc(QC)
    mr_row = R.rw.row_alloc(QC)
    r1_row = R.rw.row_alloc(QC) if h1_dram is not None else None
    mr1_row = R.rw.row_alloc(QC) if h1_dram is not None else None
    s1 = R.rw.row_alloc(QC)
    s2 = R.rw.row_alloc(QC)
    m_row = R.rw.row_alloc(QC)
    v_row = R.rw.row_alloc(QC)
    BS, BQ = R.pb[4], R.pb[5]
    BR, BM = R.pb[2], R.pb[3]
    hb = [R.xb.alloc(QC) for _ in range(2)]
    mb = R.xb.alloc(QC) if mult_dram is not None else None
    mb_ve = [0]
    hb_store = [0, 0]
    h1b = [R.fb.alloc(QC), R.fb.alloc(QC)] if h1_dram is not None else None
    h1_store = [0, 0]
    mmM_prev = 0
    prev = {"panel_users_ve": 0, "panel_users_mm": 0, "panel_users_ac": 0}

    for qc in range(NQC):
        sl = slice(qc * QC, (qc + 1) * QC)
        dgv = 0
        xbf = x_dram.tensor.dtype == BF16
        dsem = "dg" if xbf else "ds"
        for c in range(NCH):
            f = P.gdma if xbf else P.sdma
            dgv = f(xt[c], x_dram[c * 128:(c + 1) * 128, sl],
                    waits=[("ve", prev["panel_users_ve"]),
                           ("mm", prev["panel_users_mm"]),
                           ("ac", prev["panel_users_ac"])])
        mmS = None
        for c in range(NCH):
            w = [(dsem, dgv)] + (bank_waits(R, BS) if c == 0 else [])
            mmS = P.mm(BS[0:1, :], R.onesf, xt[c], start=(c == 0), stop=(c == NCH - 1),
                       waits=w, inc=("mm", 1))
        acq_mm = {}
        mmQ = None
        for c in range(NCH):
            w = [(dsem, dgv)]
            if c >= 2:
                w.append(("mm", acq_mm[c - 2]))
            if c == 0:
                w += bank_waits(R, BQ)
            acv = P.act(sq[c % 2], xt[c], AF.Square, waits=w, inc=("ac", 1))
            mmQ = P.mm(BQ[0:1, :], R.onesf, sq[c % 2], start=(c == 0), stop=(c == NCH - 1),
                       waits=[("ac", acv)], inc=("mm", 1))
            acq_mm[c] = mmQ
        vev = P.ts(m_row, BS[0:1, :], 1.0 / D, ALU.mult,
                   waits=[("mm", mmS), ("mm", mmM_prev)])
        vev = P.ts(s1, BQ[0:1, :], 1.0 / D, ALU.mult, waits=[("mm", mmQ)], inc=("ve", 1))
        R.bank_set(BS, "ve", vev)
        R.bank_set(BQ, "ve", vev)
        P.tt(s2, m_row, m_row, ALU.mult)
        vev = P.tt(v_row, s1, s2, ALU.subtract, inc=("ve", 1))
        if var_dram is not None:
            P.sdma(var_dram[0:1, sl], v_row, waits=[("ve", vev)])
        acv = P.act(s1, v_row, AF.Ln, bias=LN_EPS, waits=[("ve", vev)])
        acv = P.act(r_row, s1, AF.Exp, scale=-0.5, inc=("ac", 1))
        if h1_dram is not None:
            P.vcopy(r1_row, r_row, waits=[("ac", acv)])
            P.tt(mr1_row, m_row, r_row, ALU.mult)
        if double_ln:
            vev = P.tt(s2, r_row, r_row, ALU.mult,
                       waits=[("ac", acv)])
            vev = P.tt(s2, v_row, s2, ALU.mult, inc=("ve", 1))
            acv = P.act(s1, s2, AF.Ln, bias=LN_EPS, waits=[("ve", vev)])
            acv = P.act(s1, s1, AF.Exp, scale=-0.5, inc=("ac", 1))
            vev = P.tt(r_row, r_row, s1, ALU.mult,
                       waits=[("ac", acv)], inc=("ve", 1))
        if rms_extra:
            vev = P.tt(s2, r_row, r_row, ALU.mult, waits=[("ac", acv)])
            vev = P.tt(s2, v_row, s2, ALU.mult, inc=("ve", 1))
            acv = P.act(s1, s2, AF.Ln, bias=1e-30, waits=[("ve", vev)])
            acv = P.act(s1, s1, AF.Exp, scale=0.5, inc=("ac", 1))
            P.ts(s1, s1, 1e-8, ALU.add, waits=[("ac", acv)])
            P.vrecip(s2, s1)
            vev = P.tt(r_row, r_row, s2, ALU.mult, inc=("ve", 1))
            acv_w = [("ve", vev)]
        vev = P.tt(mr_row, m_row, r_row, ALU.mult,
                   waits=[("ac", acv)], inc=("ve", 1))
        mmR = P.mm(BR[0:128, 0:512], R.onesrow_f, r_row, True, True,
                   waits=[("ve", vev)] + bank_waits(R, BR), inc=("mm", 1))
        mmM = P.mm(BM[0:128, 0:512], R.onesrow_f, mr_row, True, True,
                   waits=bank_waits(R, BM), inc=("mm", 1))
        if h1_dram is not None:
            BR1, BM1 = R.pb[0], R.pb[1]
            P.mm(BR1[0:128, 0:512], R.onesrow_f, r1_row, True, True,
                 waits=bank_waits(R, BR1), inc=("mm", 1))
            mmM = P.mm(BM1[0:128, 0:512], R.onesrow_f, mr1_row, True, True,
                       waits=bank_waits(R, BM1), inc=("mm", 1))
        mmM_prev = mmM
        vev_last = 0
        for c in range(NCH):
            P.tt(tmp, xt[c], BR[0:128, :], ALU.mult, waits=[("mm", mmM)])
            ob = hb[c % 2]
            wst = [("ds", hb_store[c % 2])] if hb_store[c % 2] else []
            dvm = 0
            if mult_dram is not None:
                dvm = P.sdma(mb, mult_dram[c * 128:(c + 1) * 128, sl],
                             waits=[("ve", mb_ve[0])] if mb_ve[0] else [])
                wst = wst + [("ds", dvm)]
            if gb is None and mult_dram is None:
                vv = P.tt(ob, tmp, BM[0:128, :], ALU.subtract, waits=wst, inc=("ve", 1))
            elif gb is None:
                P.tt(tmp, tmp, BM[0:128, :], ALU.subtract)
                vv = P.tt(ob, tmp, mb, ALU.mult, waits=wst, inc=("ve", 1))
                mb_ve[0] = vv
            else:
                P.tt(tmp, tmp, BM[0:128, :], ALU.subtract)
                if mult_dram is None:
                    vv = P.ts(ob, tmp, gb[0][0:128, c:c + 1], ALU.mult,
                              s2=gb[1][0:128, c:c + 1], op1=ALU.add, waits=wst,
                              inc=("ve", 1))
                else:
                    P.ts(tmp, tmp, gb[0][0:128, c:c + 1], ALU.mult,
                         s2=gb[1][0:128, c:c + 1], op1=ALU.add)
                    vv = P.tt(ob, tmp, mb, ALU.mult, waits=wst, inc=("ve", 1))
                    mb_ve[0] = vv
            hb_store[c % 2] = P.sdma(h_dram[c * 128:(c + 1) * 128, sl], ob,
                                     waits=[("ve", vv)])
            if h1_dram is not None:
                P.tt(tmp, xt[c], BR1[0:128, :], ALU.mult)
                vv = P.tt(h1b[c % 2], tmp, BM1[0:128, :], ALU.subtract,
                          waits=[("ds", h1_store[c % 2])] if h1_store[c % 2] else [],
                          inc=("ve", 1))
                h1_store[c % 2] = P.sdma(h1_dram[c * 128:(c + 1) * 128, sl],
                                         h1b[c % 2], waits=[("ve", vv)])
            vev_last = vv
        R.bank_set(BR, "ve", vev_last)
        R.bank_set(BM, "ve", vev_last)
        if h1_dram is not None:
            R.bank_set(BR1, "ve", vev_last)
            R.bank_set(BM1, "ve", vev_last)
        prev = {"panel_users_ve": vev_last, "panel_users_mm": max(mmS, mmQ),
                "panel_users_ac": P.n("ac")}
    R.fb.pop()
    R.xb.pop()
    R.rw.pop()
    return {}




def load_weight_lhsT(P, R, w_dram, nfi, nfo, arena=None, fo0=0, prealloc=None,
                     fo_size=128):
    a = arena or R.wb
    fs = fo_size
    w = prealloc if prealloc is not None else a.alloc(nfi * nfo * fs)
    dg = 0
    w0 = [("mm", P.n("mm"))]
    for fi in range(nfi):
        dg = P.gdma(w[0:128, fi * nfo * fs:(fi + 1) * nfo * fs],
                    w_dram[fi * 128:(fi + 1) * 128, fo0 * fs:(fo0 + nfo) * fs],
                    waits=w0)
        w0 = []
    def tile(fi, fo):
        return w[0:128, fi * nfo * fs + fo * fs: fi * nfo * fs + (fo + 1) * fs]
    return tile, dg


def ap_cols(ap):
    s = ap.shape
    s = s() if callable(s) else s
    return s[-1]


def stage_panel_bf(P, R, x_dram, nfi, qc, arena=None, prev_users=()):
    a = arena or R.xb
    tiles = [a.alloc(QC) for _ in range(nfi)]
    dgv = 0
    for c in range(nfi):
        dgv = P.sdma(tiles[c], x_dram[c * 128:(c + 1) * 128, qc * QC:(qc + 1) * QC],
                     waits=prev_users)
    return tiles, dgv


def proj_pass(P, R, w_dram, x_dram, nfi, nfo, evac, banks=(0, 1, 2, 3),
              skip_zero=None, x_tiles=None):
    ngmax = max(1, (R.wb.cols - R.wb.off) // (nfi * 128))
    groups = []
    f = 0
    while f < nfo:
        groups.append((f, min(ngmax, nfo - f)))
        f += ngmax
    for (fo0, ng) in groups:
        R.wb.push()
        R.xb.push()
        wt, dgw = load_weight_lhsT(P, R, w_dram, nfi, ng, fo0=fo0)
        xt = None
        if x_tiles is None:
            xt = [R.xb.alloc(QC) for _ in range(nfi)]
        prev_mm = P.n("mm")
        for qc in range(NQC):
            if x_tiles is None:
                dgx = 0
                xw = []
                for c in range(nfi):
                    dgx = P.sdma(xt[c], x_dram[c * 128:(c + 1) * 128,
                                               qc * QC:(qc + 1) * QC],
                                 waits=[("mm", prev_mm)])
                tiles = xt
            else:
                tiles, xw = x_tiles(qc)
                dgx = 0
            for fol in range(ng):
                fo = fo0 + fol
                bank = R.pb[banks[(qc * nfo + fo) % len(banks)]]
                fis = [fi for fi in range(nfi)
                       if not (skip_zero and skip_zero(fi, fo))]
                mmv = None
                for j, fi in enumerate(fis):
                    w = []
                    if j == 0:
                        w = [("dg", dgw), ("ds", dgx)] + bank_waits(R, bank) + list(xw)
                    mmv = P.mm(bank[0:128, 0:512], wt(fi, fol), tiles[fi],
                               start=(j == 0), stop=(j == len(fis) - 1), waits=w,
                               inc=("mm", 1) if j == len(fis) - 1 else None)
                sem, val = evac(fo, qc, bank, mmv)
                R.bank_set(bank, sem, val)
            prev_mm = P.n("mm")
        R.xb.pop()
        R.wb.pop()





def causal_mask_tiles(P, R):
    if getattr(R, "_cmask", None) is None:
        tiles = []
        for j in range(4):
            m = (np.arange(512)[None, :] >= (j * 128 + np.arange(128))[:, None])
            t = R.cb.alloc(512)
            P.gdma(t, P.const(m.astype(np.float32), f"cm{j}")[:, :])
            tiles.append(t)
        R._cmask = tiles
    return R._cmask


def mha64_pass(P, R, qT, kT, vT, out_dram, n_heads, d_head, causal, scale,
               q_rows=None, k_rows=None):
    dh = d_head
    nkb = T // 128
    masks = causal_mask_tiles(P, R) if causal else None
    R.xb.push(); R.rw.push()
    k_aug = R.xb.alloc(T, parts=65)
    q_aug = R.xb.alloc(T, parts=65)
    sqt = R.xb.alloc(T, parts=64)
    v_tmp = R.xb.alloc(T, parts=64)
    v_aug = R.xb.alloc(nkb * 65)
    ptile = [R.xb.alloc(QC) for _ in range(3)]
    rb_sb = R.xb.alloc(QC)
    out_sb = [R.xb.alloc(QC, parts=64) for _ in range(2)]
    negc_row = R.xb.alloc(T, parts=1)
    rnb_row = R.xb.alloc(QC, parts=1)
    kn_row = R.rw.row_alloc(T)
    rn_f32 = R.rw.row_alloc(QC)
    qn_row = R.rw.row_alloc(T)
    kap_row = R.rw.row_alloc(1)
    Bs = [R.pb[0], R.pb[1], R.pb[5]]
    Bav = [R.pb[2], R.pb[3]]
    Bbc = R.pb[4]
    Bnring = [R.pb[5], R.pb[4], R.pb[3]]
    echo_t = R.rw.row_alloc(1)
    P.vmemset(k_aug[64:65, :], 1.0)
    ve0 = P.vmemset(v_aug, 1.0, inc=("ve", 1))
    out_st = [0, 0]
    ptile_mm = [0, 0, 0]

    def seg_rows(rows_fn, h, default_t):
        if rows_fn is None:
            return [(default_t, h * dh, dh)]
        segs = rows_fn(h)
        return [(s if len(s) == 3 else (default_t,) + tuple(s)) for s in segs]

    for h in range(n_heads):
        R.rw.push()
        dK = dQ = dV = 0
        r0 = 0
        for (src_t, a, n) in seg_rows(k_rows, h, kT):
            dK = P.sdma(k_aug[r0:r0 + n, :], src_t[a:a + n, :],
                        waits=[("mm", P.n("mm")), ("ac", P.n("ac"))])
            r0 += n
        r0 = 0
        for (src_t, a, n) in seg_rows(q_rows, h, qT):
            dQ = P.sdma(q_aug[r0:r0 + n, :], src_t[a:a + n, :],
                        waits=[("mm", P.n("mm")), ("ve", P.n("ve"))])
            r0 += n
        dV = P.sdma(v_tmp[0:dh, :], vT[h * dh:(h + 1) * dh, :],
                    waits=[("mm", P.n("mm"))])
        acq = P.act(q_aug[0:64, :], q_aug[0:64, :], AF.Copy, scale=float(scale),
                    waits=[("ds", dQ)], inc=("ac", 1))
        mmn = 0
        for (name, src_ap, dsem, dval, row) in (
                ("k", k_aug, "ds", dK, kn_row), ("q", q_aug, "ac", acq, qn_row)):
            acs = P.act(sqt[0:dh, :], src_ap[0:dh, :], AF.Square,
                        waits=[(dsem, dval), ("mm", mmn)], inc=("ac", 1))
            for i in range(NQC):
                Bni = Bnring[i % 3]
                w = [("ac", acs)] + bank_waits(R, Bni)
                mmn = P.mm(Bni[0:1, :], R.onesb[0:dh, 0:1], sqt[0:dh, i * QC:(i + 1) * QC],
                           True, True, waits=w, inc=("mm", 1))
                vv = P.vcopy(row[0:1, i * QC:(i + 1) * QC], Bni[0:1, :],
                             waits=[("mm", mmn)], inc=("ve", 1))
                R.bank_set(Bni, "ve", vv)
        acn = P.act(kn_row, kn_row, AF.Sqrt, waits=[("ve", P.n("ve"))])
        acn = P.act(qn_row, qn_row, AF.Sqrt, inc=("ac", 1))
        vkap = P.vred(kap_row, kn_row, ALU.max, waits=[("ac", acn)], inc=("ve", 1))
        P.ts(negc_row, qn_row, kap_row[0:1, 0:1], ALU.mult,
             s2=-float(scale), op1=ALU.mult, waits=[("ve", vkap)])
        vnc = P.vcopy(q_aug[64:65, :], negc_row, inc=("ve", 1))
        dnc = 0
        vv = P.vmemset(v_aug, 1.0, waits=[("mm", P.n("mm"))], inc=("ve", 1))
        for kb in range(nkb):
            ptb = R.pt[kb % 2]
            mmv = P.trans(ptb[0:128, 0:dh], v_tmp[0:dh, kb * 128:(kb + 1) * 128],
                          R.identb[0:dh, 0:dh],
                          waits=[("ds", dV), ("ve", vv)] + bank_waits(R, ptb),
                          inc=("mm", 1))
            P.vcopy(v_aug[0:128, kb * 65:kb * 65 + dh], ptb[0:128, 0:dh],
                    waits=[("mm", mmv)])
            vv2 = P.vcopy(echo_t, v_aug[0:1, kb * 65:kb * 65 + 1], inc=("ve", 1))
            R.bank_set(ptb, "ve", vv2)
        ve_vaug = P.n("ve")
        for qc in range(NQC):
            kb_n = 4 * (qc + 1) if causal else nkb
            bav = Bav[qc % 2]
            for kb in range(kb_n):
                bs = Bs[kb % 3]
                w = [("ve", max(vnc, ve_vaug))] + bank_waits(R, bs)
                mms = P.mm(bs[0:128, 0:512], k_aug[:, kb * 128:(kb + 1) * 128],
                           q_aug[:, qc * QC:(qc + 1) * QC], True, True,
                           waits=w, inc=("mm", 1))
                pt_ = ptile[kb % 3]
                acp = P.act(pt_, bs[0:128, 0:512], AF.Exp,
                            waits=[("mm", mms)] + ([("mm", ptile_mm[kb % 3])]
                                                   if ptile_mm[kb % 3] else []),
                            inc=("ac", 1))
                R.bank_set(bs, "ac", acp)
                dep = ("ac", acp)
                if causal and kb >= kb_n - 4:
                    j = kb - (kb_n - 4)
                    vmk = P.tt(pt_, pt_, masks[j], ALU.mult, waits=[dep], inc=("ve", 1))
                    dep = ("ve", vmk)
                w2 = [dep]
                if kb == 0:
                    w2 += bank_waits(R, bav)
                mma = P.mm(bav[0:65, 0:512], v_aug[:, kb * 65:(kb + 1) * 65], pt_,
                           kb == 0, kb == kb_n - 1, waits=w2,
                           inc=("mm", 1))
                ptile_mm[kb % 3] = mma
            P.vrecip(rn_f32, bav[64:65, 0:512], waits=[("mm", mma)])
            vrc = P.vcopy(rnb_row, rn_f32, inc=("ve", 1))
            mmb = P.mm(Bbc[0:128, 0:512], R.onesrow_b, rnb_row, True, True,
                       waits=[("ve", vrc)] + bank_waits(R, Bbc), inc=("mm", 1))
            vcb = P.vcopy(rb_sb, Bbc[0:128, 0:512], waits=[("mm", mmb)], inc=("ve", 1))
            R.bank_set(Bbc, "ve", vcb)
            ob = out_sb[qc % 2]
            wst = [("ds", out_st[qc % 2])] if out_st[qc % 2] else []
            vout = P.tt(ob, bav[0:64, 0:512], rb_sb[0:64, :], ALU.mult,
                        waits=wst, inc=("ve", 1))
            R.bank_set(bav, "ve", vout)
            out_st[qc % 2] = P.sdma(
                out_dram[h * dh:(h + 1) * dh, qc * QC:(qc + 1) * QC], ob[0:dh, :],
                waits=[("ve", vout)])
        R.rw.pop()
    R.xb.pop(); R.rw.pop()




def stage_colvec(P, R, dram_2d, n):
    t = R.rw.alloc(n)
    dv = P.gdma(t, dram_2d[:, 0:n])
    return t, dv


def mk_evac_store(P, R, out_dram, func=None, bias_tile=None, scale=None, dvb=0,
                  out_dt_tiles=None, scale_fn=None, func_fn=None):
    ob = out_dt_tiles or [R.xb.alloc(QC), R.xb.alloc(QC)]
    st = [P.n("ds"), P.n("ds")]
    i_ctr = [0]

    def evac(fo, qc, bank, mmv):
        i = i_ctr[0] % 2
        i_ctr[0] += 1
        w = [("mm", mmv)] + ([("ds", st[i])] if st[i] else []) + ([("dg", dvb)] if bias_tile is not None else [])
        b = bias_tile[0:128, fo:fo + 1] if bias_tile is not None else None
        f = (func_fn(fo) if func_fn is not None else func) or (
            AF.Identity if b is not None else AF.Copy)
        sc = scale_fn(fo) if scale_fn is not None else scale
        acv = P.act(ob[i], bank[0:128, 0:512], f, bias=b, scale=sc,
                    waits=w, inc=("ac", 1))
        st[i] = P.sdma(out_dram[fo * 128:(fo + 1) * 128, qc * QC:(qc + 1) * QC],
                       ob[i], waits=[("ac", acv)])
        return ("ac", acv)
    return evac


def mk_evac_residual(P, R, xin_dram, xout_dram, bias_tile=None, dvb=0,
                     extra_dram=None):
    xt = [R.fb.alloc(QC), R.fb.alloc(QC)]
    et = [R.fb.alloc(QC), R.fb.alloc(QC)] if extra_dram is not None else None
    ot = [R.fb.alloc(QC), R.fb.alloc(QC)]
    st = [P.n("ds"), P.n("ds")]
    ld = [0, 0]
    i_ctr = [0]

    def evac(fo, qc, bank, mmv):
        i = i_ctr[0] % 2
        i_ctr[0] += 1
        wl = [("ve", ld[i])] if ld[i] else []
        dvx = P.sdma(xt[i], xin_dram[fo * 128:(fo + 1) * 128, qc * QC:(qc + 1) * QC],
                     waits=wl)
        if extra_dram is not None:
            dvx = P.sdma(et[i], extra_dram[fo * 128:(fo + 1) * 128,
                                           qc * QC:(qc + 1) * QC])
        w = [("mm", mmv), ("ds", max(dvx, st[i]))] + ([("dg", dvb)] if bias_tile is not None else [])
        if bias_tile is not None:
            vv = P.stt(ot[i], bank[0:128, 0:512], bias_tile[0:128, fo:fo + 1],
                       ALU.add, xt[i], ALU.add, waits=w, inc=("ve", 1))
        else:
            vv = P.tt(ot[i], bank[0:128, 0:512], xt[i], ALU.add, waits=w, inc=("ve", 1))
        if extra_dram is not None:
            vv = P.tt(ot[i], ot[i], et[i], ALU.add, inc=("ve", 1))
        ld[i] = vv
        st[i] = P.sdma(xout_dram[fo * 128:(fo + 1) * 128, qc * QC:(qc + 1) * QC],
                       ot[i], waits=[("ve", vv)])
        return ("ve", vv)
    return evac




def emit_l3_attn_a(P, R, xin, xout, d):
    h = P.dram_t("s_h", [D, T], BF16)
    qkvT = P.dram_t("s_qkv", [3 * D, T], BF16)
    att = P.dram_t("s_att", [D, T], BF16)
    ln_pass(P, R, xin, h)
    P.barrier()
    R.rw.push(); R.xb.push()
    bqkv, dvb = stage_colvec(P, R, d["b_qkv2d"], 24)
    evac = mk_evac_store(P, R, qkvT, bias_tile=bqkv, dvb=dvb)
    proj_pass(P, R, d["W_qkv"], h, nfi=8, nfo=24, evac=evac)
    R.xb.pop(); R.rw.pop()
    P.barrier()
    mha64_pass(P, R, qkvT[0:D, :], qkvT.tensor[D:2 * D, :] if False else qkvT[D:2 * D, :],
               qkvT[2 * D:3 * D, :], att, 16, 64, False, 0.125)
    P.barrier()
    R.rw.push(); R.fb.push()
    bo, dvo = stage_colvec(P, R, d["b_o2d"], 8)
    evac2 = mk_evac_residual(P, R, xin, xout, bias_tile=bo, dvb=dvo)
    proj_pass(P, R, d["W_o"], att, nfi=8, nfo=8, evac=evac2)
    R.fb.pop(); R.rw.pop()
    P.barrier()




def rope_interleaved_tables(S, d_rope):
    inv_freq = 1.0 / 10000.0 ** (np.arange(0, d_rope, 2, dtype=np.float32) / d_rope)
    t = np.arange(S, dtype=np.float32)
    freqs = np.outer(t, inv_freq)
    cos = np.concatenate([np.cos(freqs), np.cos(freqs)], -1)[:, ::2]
    sin = np.concatenate([np.sin(freqs), np.sin(freqs)], -1)[:, ::2]
    return cos.astype(np.float32), sin.astype(np.float32)


def rope_pass_interleaved(P, R, src_dram, dst_dram, nch=8):
    cos, sin = rope_interleaved_tables(T, 64)
    cosT = np.ascontiguousarray(np.tile(cos.T, (2, 1)))
    sinT = np.ascontiguousarray(np.tile(sin.T, (2, 1)))
    R.xb.push()
    ct = R.xb.alloc(T, parts=64)
    st_ = R.xb.alloc(T, parts=64)
    ch = R.xb.alloc(T)
    oc = R.xb.alloc(T, parts=64)
    t1 = R.xb.alloc(T, parts=64)
    ob = R.xb.alloc(T)
    dc = P.gdma(ct, P.const(cosT, "ropec")[:, :])
    dc = P.gdma(st_, P.const(sinT, "ropes")[:, :])
    stv = 0
    for c in range(nch):
        dv = P.sdma(ch, src_dram[c * 128:(c + 1) * 128, :],
                    waits=[("ve", P.n("ve"))])
        w = [("ds", max(dv, stv)), ("dg", dc)]
        P.vcopy(oc, ch[64:128, 0:T], waits=w)
        P.tt(t1, oc, st_, ALU.mult)
        P.tt(ob[0:64, :], ch[0:64, :], ct, ALU.mult)
        P.tt(ob[0:64, :], ob[0:64, :], t1, ALU.subtract)
        P.tt(t1, oc, ct, ALU.mult)
        P.tt(oc, ch[0:64, :], st_, ALU.mult)
        P.tt(oc, oc, t1, ALU.add)
        vv = P.vcopy(ob[64:128, 0:T], oc, inc=("ve", 1))
        stv = P.sdma(dst_dram[c * 128:(c + 1) * 128, :], ob, waits=[("ve", vv)])
    R.xb.pop()


def l0b_rows(h):
    c = h // 2
    return [(c * 128 + (h % 2) * 32, 32), (c * 128 + 64 + (h % 2) * 32, 32)]


def emit_l0_attn_b(P, R, xin, xout, d):
    h2 = P.dram_t("s_h", [D, T], BF16)
    cq = P.dram_t("s_cq", [512, T], BF16)
    ckv = P.dram_t("s_ckv", [512, T], BF16)
    qT = P.dram_t("s_q", [D, T], BF16)
    kT = P.dram_t("s_k", [D, T], BF16)
    vT = P.dram_t("s_v", [D, T], BF16)
    att = P.dram_t("s_att", [D, T], BF16)
    h1 = P.dram_t("s_h1", [D, T], F32)
    ln_pass(P, R, xin, h2, gamma_beta=(d["ln_scale"], d["ln_bias"]), double_ln=True,
            h1_dram=h1)
    P.barrier()
    R.xb.push()
    ev = mk_evac_store(P, R, cq)
    proj_pass(P, R, d["w_down_qT"], h2, nfi=8, nfo=4, evac=ev, banks=(0, 1, 2, 3))
    ev = mk_evac_store(P, R, ckv)
    proj_pass(P, R, d["w_down_kvT"], h2, nfi=8, nfo=4, evac=ev, banks=(0, 1, 2, 3))
    P.barrier()
    R.xb.pop(); R.xb.push()
    ev = mk_evac_store(P, R, qT)
    proj_pass(P, R, d["w_up_qTp"], cq, nfi=4, nfo=8, evac=ev)
    ev = mk_evac_store(P, R, kT)
    proj_pass(P, R, d["w_up_kTp"], ckv, nfi=4, nfo=8, evac=ev)
    ev = mk_evac_store(P, R, vT)
    proj_pass(P, R, d["w_up_vT"], ckv, nfi=4, nfo=8, evac=ev)
    P.barrier()
    R.xb.pop()
    rope_pass_interleaved(P, R, qT, qT)
    rope_pass_interleaved(P, R, kT, kT)
    P.barrier()
    mha64_pass(P, R, qT, kT, vT, att, 16, 64, False, 0.125,
               q_rows=l0b_rows, k_rows=l0b_rows)
    P.barrier()
    R.fb.push()
    ev = mk_evac_residual(P, R, xin, xout, extra_dram=h1)
    proj_pass(P, R, d["w_outT"], att, nfi=8, nfo=8, evac=ev)
    R.fb.pop()
    P.barrier()




def ternary_build(P, R, wT_dram, wt_dram, nfi, nfo128):
    R.fb.push(); R.rw.push(); R.xb.push()
    ncols = nfo128 * 128
    CHN = min(ncols, 2048)
    nch = ncols // CHN
    wch = R.fb.alloc(CHN)
    ach = R.fb.alloc(CHN)
    srow = R.rw.row_alloc(QC)
    tot = R.rw.row_alloc(1)
    ahc = R.rw.alloc(1)
    sgn = R.xb.alloc(CHN)
    msk = R.xb.alloc(CHN)
    wt = R.xb.alloc(CHN)
    Bn = R.pb[4]
    mmn = 0
    for fi in range(nfi):
        for ch in range(nch):
            csl = slice(ch * CHN, (ch + 1) * CHN)
            dv = P.gdma(wch, wT_dram[fi * 128:(fi + 1) * 128, csl],
                        waits=[("ac", P.n("ac")), ("mm", P.n("mm"))])
            aca = P.act(ach, wch, AF.Abs, waits=[("dg", dv), ("mm", mmn)],
                        inc=("ac", 1))
            for j in range(CHN // QC):
                first = fi == 0 and ch == 0 and j == 0
                last = fi == nfi - 1 and ch == nch - 1 and j == CHN // QC - 1
                mmn = P.mm(Bn[0:1, :], R.onesf, ach[0:128, j * QC:(j + 1) * QC],
                           first, last,
                           waits=[("ac", aca)] + (bank_waits(R, Bn) if first else []),
                           inc=("mm", 1))
    vv = P.vcopy(srow, Bn[0:1, :], waits=[("mm", mmn)], inc=("ve", 1))
    R.bank_set(Bn, "ve", vv)
    P.vred(tot, srow, ALU.add)
    P.ts(tot, tot, 0.5 / (nfi * 128 * ncols), ALU.mult, inc=("ve", 1))
    Bb = R.pb[5]
    mmb = P.mm(Bb[0:128, 0:1], R.onesrow_f, tot[0:1, 0:1], True, True,
               waits=[("ve", P.n("ve"))] + bank_waits(R, Bb), inc=("mm", 1))
    vah = P.vcopy(ahc, Bb[0:128, 0:1], waits=[("mm", mmb)], inc=("ve", 1))
    R.bank_set(Bb, "ve", vah)
    stv = 0
    vlast = P.n("ve")
    for fi in range(nfi):
        for ch in range(nch):
            csl = slice(ch * CHN, (ch + 1) * CHN)
            dv = P.gdma(wch, wT_dram[fi * 128:(fi + 1) * 128, csl],
                        waits=[("ac", P.n("ac")), ("ve", P.n("ve"))])
            aca = P.act(ach, wch, AF.Abs, waits=[("dg", dv), ("ve", vlast)],
                        inc=("ac", 1))
            acs = P.act(sgn, wch, AF.Sign, inc=("ac", 1))
            P.ts(msk, ach, ahc[0:128, 0:1], ALU.is_gt,
                 waits=[("ac", acs), ("ds", stv)])
            vv = P.tt(wt, sgn, msk, ALU.mult, inc=("ve", 1))
            vlast = vv
            stv = P.sdma(wt_dram[fi * 128:(fi + 1) * 128, csl], wt,
                         waits=[("ve", vv)])
    R.fb.pop(); R.rw.pop(); R.xb.pop()


def emit_l0_ffn(P, R, xin, xout, d, first_call):
    h = P.dram_t("s_h", [D, T], BF16)
    g = P.dram_t("s_g", [4096, T], BF16)
    wtu = P.dram_t("s_wtu", [D, 4096], BF16)
    wtd = P.dram_t("s_wtd", [4096, D], BF16)
    if first_call:
        ternary_build(P, R, d["w_upT"], wtu, 8, 32)
        P.barrier()
        ternary_build(P, R, d["w_downT"], wtd, 32, 8)
        P.barrier()
    ln_pass(P, R, xin, h, rms_extra=True)
    P.barrier()
    R.xb.push(); R.rw.push(); R.fb.push()
    ev = mk_evac_store(P, R, g, func=AF.Gelu)
    proj_pass(P, R, wtu, h, nfi=8, nfo=32, evac=ev)
    P.barrier()
    Bn = R.pb[4]
    Bb = R.pb[5]
    gb = [R.xb.alloc(QC) for _ in range(32)]
    sqb = R.xb.alloc(QC)
    s2b = R.fb.alloc(QC)
    s2r = R.rw.row_alloc(QC)
    ev = mk_evac_residual(P, R, xin, xout)

    def x_tiles(qc):
        mmn = None
        for c in range(32):
            dv = P.sdma(gb[c], g[c * 128:(c + 1) * 128, qc * QC:(qc + 1) * QC],
                        waits=[("mm", P.n("mm")), ("ve", P.n("ve"))] if c == 0 else [])
            acq = P.act(sqb, gb[c], AF.Square,
                        waits=[("ds", dv), ("mm", mmn if mmn else 0)], inc=("ac", 1))
            mmn = P.mm(Bn[0:1, :], R.onesb[0:128, 0:1], sqb, c == 0, c == 31,
                       waits=[("ac", acq)] + (bank_waits(R, Bn) if c == 0 else []),
                       inc=("mm", 1))
        P.ts(s2r, Bn[0:1, :], 1.0 / 4096, ALU.mult, waits=[("mm", mmn)])
        vv = P.vcopy(s2r, s2r, inc=("ve", 1))
        R.bank_set(Bn, "ve", vv)
        acv2 = P.act(s2r, s2r, AF.Ln, bias=1e-30, waits=[("ve", vv)])
        acv2 = P.act(s2r, s2r, AF.Exp, scale=0.5, inc=("ac", 1))
        P.ts(s2r, s2r, 1e-8, ALU.add, waits=[("ac", acv2)])
        vv = P.vrecip(s2r, s2r, inc=("ve", 1))
        mmb = P.mm(Bb[0:128, 0:512], R.onesrow_f, s2r, True, True,
                   waits=[("ve", vv)] + bank_waits(R, Bb), inc=("mm", 1))
        vcb = P.vcopy(s2b, Bb[0:128, 0:512], waits=[("mm", mmb)], inc=("ve", 1))
        R.bank_set(Bb, "ve", vcb)
        vv = 0
        for c in range(32):
            vv = P.tt(gb[c], gb[c], s2b, ALU.mult,
                      inc=("ve", 1) if c == 31 else None)
        return gb, [("ve", vv)]
    proj_pass(P, R, wtd, None, nfi=32, nfo=8, evac=ev, x_tiles=x_tiles)
    R.xb.pop(); R.rw.pop(); R.fb.pop()
    P.barrier()




def emit_l1_ffn(P, R, xin, xout, d, idx=""):
    h = P.dram_t("s_h", [D, T], BF16)
    xr = P.dram_t("s_xr", [D, T], BF16)
    xk = P.dram_t("s_xk", [D, T], BF16)
    rr = P.dram_t("s_rr", [D, T], BF16)
    k2 = P.dram_t("s_k2", [D, T], BF16)
    ln_pass(P, R, xin, h)
    P.barrier()
    R.xb.push(); R.rw.push()
    mu_r, dv1 = stage_colvec(P, R, d["mu_r2d"], 8)
    mu_k, dv2 = stage_colvec(P, R, d["mu_k2d"], 8)
    om_r = R.rw.alloc(8)
    om_k = R.rw.alloc(8)
    vv0 = P.ts(om_r, mu_r, -1.0, ALU.mult, s2=1.0, op1=ALU.add,
               waits=[("dg", dv2)])
    vv0 = P.ts(om_k, mu_k, -1.0, ALU.mult, s2=1.0, op1=ALU.add, inc=("ve", 1))
    ch = R.xb.alloc(T + 1)
    ob = R.xb.alloc(T)
    ob2 = R.xb.alloc(T)
    P.vmemset(ch[0:128, 0:1], 0.0, inc=("ve", 1))
    st1 = st2 = 0
    for c in range(NCH):
        dv = P.sdma(ch[0:128, 1:T + 1], h[c * 128:(c + 1) * 128, :],
                    waits=[("ve", P.n("ve"))])
        w = [("ds", max(dv, st1, st2)), ("ve", vv0)]
        P.ts(ob, ch[0:128, 1:T + 1], mu_r[0:128, c:c + 1], ALU.mult, waits=w)
        vv = P.stt(ob, ch[0:128, 0:T], om_r[0:128, c:c + 1], ALU.mult, ob, ALU.add,
                   inc=("ve", 1))
        st1 = P.sdma(xr[c * 128:(c + 1) * 128, :], ob, waits=[("ve", vv)])
        P.ts(ob2, ch[0:128, 1:T + 1], mu_k[0:128, c:c + 1], ALU.mult)
        vv = P.stt(ob2, ch[0:128, 0:T], om_k[0:128, c:c + 1], ALU.mult, ob2, ALU.add,
                   inc=("ve", 1))
        st2 = P.sdma(xk[c * 128:(c + 1) * 128, :], ob2, waits=[("ve", vv)])
    R.xb.pop(); R.rw.pop()
    P.barrier()
    R.xb.push()
    ev = mk_evac_store(P, R, rr, func=AF.Sigmoid)
    proj_pass(P, R, d["W_rT"], xr, nfi=8, nfo=8, evac=ev)
    ob = [R.xb.alloc(QC), R.xb.alloc(QC)]
    st = [P.n("ds"), P.n("ds")]
    ictr = [0]

    def evac_k(fo, qc, bank, mmv):
        i = ictr[0] % 2
        ictr[0] += 1
        w = [("mm", mmv)] + ([("ds", st[i])] if st[i] else [])
        P.act(ob[i], bank[0:128, 0:512], AF.Relu, waits=w)
        acv = P.act(ob[i], ob[i], AF.Square, inc=("ac", 1))
        st[i] = P.sdma(k2[fo * 128:(fo + 1) * 128, qc * QC:(qc + 1) * QC], ob[i],
                       waits=[("ac", acv)])
        return ("ac", acv)
    proj_pass(P, R, d["W_kT"], xk, nfi=8, nfo=8, evac=evac_k)
    P.barrier()
    R.xb.pop()
    R.fb.push(); R.xb.push()
    rrt = [R.xb.alloc(QC), R.xb.alloc(QC)]
    xt2 = [R.fb.alloc(QC), R.fb.alloc(QC)]
    ot = [R.fb.alloc(QC), R.fb.alloc(QC)]
    st = [P.n("ds"), P.n("ds")]
    ld = [0, 0]
    ictr2 = [0]

    def evac_v(fo, qc, bank, mmv):
        i = ictr2[0] % 2
        ictr2[0] += 1
        wl = [("ve", ld[i])] if ld[i] else []
        P.sdma(rrt[i], rr[fo * 128:(fo + 1) * 128, qc * QC:(qc + 1) * QC], waits=wl)
        dvx = P.sdma(xt2[i], xin[fo * 128:(fo + 1) * 128, qc * QC:(qc + 1) * QC])
        w = [("mm", mmv), ("ds", max(dvx, st[i]))]
        P.tt(ot[i], bank[0:128, 0:512], rrt[i], ALU.mult, waits=w)
        vv = P.tt(ot[i], ot[i], xt2[i], ALU.add, inc=("ve", 1))
        ld[i] = vv
        st[i] = P.sdma(xout[fo * 128:(fo + 1) * 128, qc * QC:(qc + 1) * QC],
                       ot[i], waits=[("ve", vv)])
        return ("ve", vv)
    proj_pass(P, R, d["W_vT"], k2, nfi=8, nfo=8, evac=evac_v)
    R.fb.pop(); R.xb.pop()
    P.barrier()




def transpose_to_tok(P, R, vT_dram, vtok_dram, nch=8):
    R.xb.push(); R.rw.push()
    vch = R.xb.alloc(T)
    bt = [R.xb.alloc(128), R.xb.alloc(128)]
    ech = R.rw.row_alloc(1)
    tt_st = [0, 0]
    for c in range(nch):
        dv = P.sdma(vch, vT_dram[c * 128:(c + 1) * 128, :],
                    waits=[("mm", P.n("mm"))])
        for kb in range(T // 128):
            ptb = R.pt[kb % 2]
            mmv = P.trans(ptb[0:128, 0:128], vch[0:128, kb * 128:(kb + 1) * 128],
                          R.identb[0:128, 0:128],
                          waits=([("ds", dv)] if kb == 0 else []) + bank_waits(R, ptb),
                          inc=("mm", 1))
            i = kb % 2
            P.vcopy(bt[i], ptb[0:128, 0:128],
                    waits=[("mm", mmv)] + ([("ds", tt_st[i])] if tt_st[i] else []))
            vv2 = P.vcopy(ech, bt[i][0:1, 0:1], inc=("ve", 1))
            R.bank_set(ptb, "ve", vv2)
            tt_st[i] = P.sdma(
                vtok_dram[kb * 128:(kb + 1) * 128, c * 128:(c + 1) * 128], bt[i],
                waits=[("ve", vv2)])
    R.xb.pop(); R.rw.pop()


def residual_add_pass(P, R, xin, add_dram, xout, add_bf16=True):
    H2 = T // 2
    R.fb.push(); R.xb.push()
    xt = R.fb.alloc(H2)
    at = R.xb.alloc(H2) if add_bf16 else R.fb.alloc(H2)
    ot = R.fb.alloc(H2)
    stv = 0
    for c in range(NCH):
        for hh in range(2):
            sl = slice(hh * H2, (hh + 1) * H2)
            dv = P.sdma(xt, xin[c * 128:(c + 1) * 128, sl], waits=[("ve", P.n("ve"))])
            dv = P.sdma(at, add_dram[c * 128:(c + 1) * 128, sl])
            vv = P.tt(ot, xt, at, ALU.add, waits=[("ds", max(dv, stv))], inc=("ve", 1))
            stv = P.sdma(xout[c * 128:(c + 1) * 128, sl], ot, waits=[("ve", vv)])
    R.fb.pop(); R.xb.pop()


def attn1024_pass(P, R, qT, kT, vtok_dram, out_dram, causal, post_A=None,
                  evac_out=None):
    nkb = T // 128
    masks = causal_mask_tiles(P, R) if causal else None
    R.xb.push(); R.rw.push(); R.fb.push(); R.wb.push()
    pt_buf = R.xb.alloc(nkb * QC)
    kst = [R.xb.alloc(1024), R.xb.alloc(1024)]
    qp = [R.xb.alloc(QC) for _ in range(NCH)]
    sqt = R.xb.alloc(QC)
    negc = R.xb.alloc(QC, parts=1)
    rnb = R.xb.alloc(QC, parts=1)
    rb_sb = R.xb.alloc(QC)
    vst = R.wb.alloc(32 * QC)
    qn_row = R.rw.row_alloc(QC)
    kn_row = R.rw.row_alloc(T)
    kap = R.rw.row_alloc(1)
    rn_f32 = R.rw.row_alloc(QC)
    s1 = R.rw.row_alloc(QC)
    Bs = [R.pb[0], R.pb[1], R.pb[2]]
    BL = R.pb[4]
    Bbc = R.pb[5]
    hooks = {"negc_ve": 0}
    for i in range(NQC):
        dv = 0
        for c in range(NCH):
            dv = P.sdma(qp[c], kT[c * 128:(c + 1) * 128, i * QC:(i + 1) * QC],
                        waits=[("mm", P.n("mm")), ("ac", P.n("ac"))])
        Bni = Bs[i % 3]
        mmn = None
        for c in range(NCH):
            acs = P.act(sqt, qp[c], AF.Square,
                        waits=[("ds", dv), ("mm", P.n("mm"))], inc=("ac", 1))
            mmn = P.mm(Bni[0:1, :], R.onesb[0:128, 0:1], sqt, c == 0, c == NCH - 1,
                       waits=[("ac", acs)] + (bank_waits(R, Bni) if c == 0 else []),
                       inc=("mm", 1))
        vv = P.vcopy(kn_row[0:1, i * QC:(i + 1) * QC], Bni[0:1, :],
                     waits=[("mm", mmn)], inc=("ve", 1))
        R.bank_set(Bni, "ve", vv)
    acn = P.act(kn_row, kn_row, AF.Sqrt, waits=[("ve", P.n("ve"))], inc=("ac", 1))
    vkap = P.vred(kap, kn_row, ALU.max, waits=[("ac", acn)], inc=("ve", 1))
    for qc in range(NQC):
        kb_n = 4 * (qc + 1) if causal else nkb
        dv = 0
        for c in range(NCH):
            dv = P.sdma(qp[c], qT[c * 128:(c + 1) * 128, qc * QC:(qc + 1) * QC],
                        waits=[("mm", P.n("mm")), ("ac", P.n("ac"))])
        Bni = Bs[qc % 3]
        mmn = None
        for c in range(NCH):
            acs = P.act(sqt, qp[c], AF.Square,
                        waits=[("ds", dv), ("mm", P.n("mm"))], inc=("ac", 1))
            mmn = P.mm(Bni[0:1, :], R.onesb[0:128, 0:1], sqt, c == 0, c == NCH - 1,
                       waits=[("ac", acs)] + bank_waits(R, Bni), inc=("mm", 1))
        P.ts(qn_row, Bni[0:1, :], 1.0, ALU.mult, waits=[("mm", mmn), ("ve", vkap)])
        vv = P.ins("vector", lambda e, o=qn_row: e.tensor_copy(o, o), inc=("ve", 1))
        R.bank_set(Bni, "ve", vv)
        acq = P.act(qn_row, qn_row, AF.Sqrt, waits=[("ve", vv)], inc=("ac", 1))
        vnc = P.ts(negc, qn_row, kap[0:1, 0:1],
                   ALU.mult, s2=-1.0, op1=ALU.mult,
                   waits=[("ac", acq), ("mm", P.n("mm"))], inc=("ve", 1))
        hooks["negc_ve"] = vnc
        kT3 = kT.rearrange("(c p) t -> p c t", p=128)
        mml = None
        acp_last = 0
        for kb in range(kb_n):
            i = kb % 2
            dk = P.sdma(kst[i].rearrange("p (c t) -> p c t", t=128),
                        kT3[:, :, kb * 128:(kb + 1) * 128],
                        waits=[("mm", P.n("mm"))])
            bs = Bs[kb % 3]
            mms = None
            for c in range(NCH):
                w = []
                if c == 0:
                    w = [("ds", dk), ("ve", hooks["negc_ve"])] + bank_waits(R, bs)
                P.mm(bs[0:128, 0:512], kst[i][0:128, c * 128:(c + 1) * 128],
                     qp[c], start=(c == 0), stop=False, waits=w)
            mms = P.mm(bs[0:128, 0:512], R.onesrow_b, negc, False, True,
                       inc=("mm", 1))
            pcol = pt_buf[0:128, kb * QC:(kb + 1) * QC]
            acp = P.act(pcol, bs[0:128, 0:512], AF.Exp, waits=[("mm", mms)],
                        inc=("ac", 1))
            R.bank_set(bs, "ac", acp)
            dep = ("ac", acp)
            if causal and kb >= kb_n - 4:
                j = kb - (kb_n - 4)
                vmk = P.tt(pcol, pcol, masks[j], ALU.mult, waits=[dep], inc=("ve", 1))
                dep = ("ve", vmk)
            mml = P.mm(BL[0:1, 0:512], R.onesb[0:128, 0:1], pcol,
                       kb == 0, kb == kb_n - 1,
                       waits=[dep] + (bank_waits(R, BL) if kb == 0 else []),
                       inc=("mm", 1))
            acp_last = acp
        P.vrecip(rn_f32, BL[0:1, 0:512], waits=[("mm", mml)])
        vrc = P.vcopy(rnb, rn_f32, inc=("ve", 1))
        R.bank_set(BL, "ve", vrc)
        mmb = P.mm(Bbc[0:128, 0:512], R.onesrow_b, rnb, True, True,
                   waits=[("ve", vrc)] + bank_waits(R, Bbc), inc=("mm", 1))
        vcb = P.vcopy(rb_sb, Bbc[0:128, 0:512], waits=[("mm", mmb)], inc=("ve", 1))
        R.bank_set(Bbc, "ve", vcb)
        hooks["rb_ve"] = vcb
        hooks["rn_f32"] = rn_f32
        hooks["pt_buf"] = pt_buf
        hooks["kb_n"] = kb_n
        if post_A is not None:
            post_A(qc, hooks)
        vtok3 = vtok_dram.rearrange("(kb p) c -> p kb c", p=128)
        for half in range(2):
            dvv = 0
            for g in range(4):
                dvv = P.sdma(
                    vst.rearrange("p (kb c) -> p kb c", c=QC)[:, g * 8:(g + 1) * 8, :],
                    vtok3[:, g * 8:(g + 1) * 8, half * QC:(half + 1) * QC],
                    waits=[("mm", P.n("mm"))] if g == 0 else [])
            for ccl in range(4):
                cc = half * 4 + ccl
                bank = R.pb[ccl]
                mma = None
                for kb in range(kb_n):
                    w = []
                    if kb == 0:
                        w = [("ds", dvv), ("ac", acp_last), ("ve", P.waited["vector"].get("nothing", 0) or 0)] + bank_waits(R, bank)
                    mma = P.mm(bank[0:128, 0:512],
                               vst[0:128, kb * QC + ccl * 128: kb * QC + ccl * 128 + 128],
                               pt_buf[0:128, kb * QC:(kb + 1) * QC],
                               start=(kb == 0), stop=(kb == kb_n - 1), waits=w,
                               inc=("mm", 1) if kb == kb_n - 1 else None)
                sem, val = (evac_out or _default_evac1024)(P, R, out_dram, cc, qc,
                                                           bank, mma, rb_sb, hooks)
                R.bank_set(bank, sem, val)
    R.xb.pop(); R.rw.pop(); R.fb.pop(); R.wb.pop()


def _default_evac1024(P, R, out_dram, cc, qc, bank, mma, rb_sb, hooks):
    if "ob" not in hooks:
        hooks["ob"] = [R.xb.alloc(QC), R.xb.alloc(QC)]
        hooks["ost"] = [P.n("ds"), P.n("ds")]
    ob = hooks["ob"]
    st = hooks["ost"]
    i = (qc * 8 + cc) % 2
    w = [("mm", mma), ("ve", hooks["rb_ve"])] + ([("ds", st[i])] if st[i] else [])
    vv = P.tt(ob[i], bank[0:128, 0:512], rb_sb, ALU.mult, waits=w, inc=("ve", 1))
    st[i] = P.sdma(out_dram[cc * 128:(cc + 1) * 128, qc * QC:(qc + 1) * QC],
                   ob[i], waits=[("ve", vv)])
    return ("ve", vv)


def emit_l1_attn_b(P, R, xin, xout, d=None):
    h = P.dram_t("s_h", [D, T], BF16)
    htok = P.dram_t("s_htok", [T, D], BF16)
    att = P.dram_t("s_att", [D, T], BF16)
    ln_pass(P, R, xin, h)
    P.barrier()
    transpose_to_tok(P, R, h, htok)
    P.barrier()
    attn1024_pass(P, R, h, h, htok, att, causal=False)
    P.barrier()
    residual_add_pass(P, R, xin, att, xout)
    P.barrier()




def emit_l0_attn_a(P, R, xin, xout, d):
    h = P.dram_t("s_h", [D, T], BF16)
    qkv = P.dram_t("s_qkv", [3 * D, T], BF16)
    CT = P.dram_t("s_ct", [D, T], BF16)
    dvnT = P.dram_t("s_dvn", [D, T], BF16)
    Ctok = P.dram_t("s_ctok", [T, D], BF16)
    att = P.dram_t("s_att", [D, T], BF16)
    ln_pass(P, R, xin, h)
    P.barrier()
    R.rw.push(); R.xb.push()
    bqkv, dvb = stage_colvec(P, R, d["b_qkv2d"], 24)
    P.ts(bqkv[0:128, 0:8], bqkv[0:128, 0:8], 1.0 / 32, ALU.mult,
         waits=[("dg", dvb)], inc=("ve", 1))
    dvb = 0

    def scfn(fo):
        return 1.0 / 32 if fo < 8 else None
    ev = mk_evac_store(P, R, qkv, bias_tile=bqkv, scale_fn=scfn)
    P.wait("scalar", "ve", P.n("ve"))
    proj_pass(P, R, d["W_qkv"], h, nfi=8, nfo=24, evac=ev)
    P.barrier()
    R.xb.pop(); R.rw.pop()
    R.xb.push(); R.rw.push()
    w0, dv0 = stage_colvec(P, R, d["W_vconv0"], 8)
    w1, dv1 = stage_colvec(P, R, d["W_vconv1"], 8)
    w2, dv2 = stage_colvec(P, R, d["W_vconv2"], 8)
    ch = R.xb.alloc(T + 2)
    tb = R.xb.alloc(T)
    db = R.xb.alloc(T)
    P.vmemset(ch[0:128, 0:1], 0.0)
    P.vmemset(ch[0:128, T + 1:T + 2], 0.0, inc=("ve", 1))
    st1 = st2 = 0
    for c in range(NCH):
        dv = P.sdma(ch[0:128, 1:T + 1], qkv[2 * D + c * 128:2 * D + (c + 1) * 128, :],
                    waits=[("ve", P.n("ve"))])
        w = [("ds", max(dv, st1, st2)), ("dg", dv2)]
        vv1 = P.ts(db, ch[0:128, 2:T + 2], w2[0:128, c:c + 1], ALU.mult, waits=w,
                   inc=("ve", 1))
        st1 = P.sdma(dvnT[c * 128:(c + 1) * 128, :], db, waits=[("ve", vv1)])
        P.ts(tb, ch[0:128, 1:T + 1], w1[0:128, c:c + 1], ALU.mult)
        P.stt(tb, ch[0:128, 0:T], w0[0:128, c:c + 1], ALU.mult, tb, ALU.add)
        vv2 = P.tt(tb, tb, db, ALU.add, inc=("ve", 1))
        st2 = P.sdma(CT[c * 128:(c + 1) * 128, :], tb, waits=[("ve", vv2)])
    R.xb.pop(); R.rw.pop()
    P.barrier()
    transpose_to_tok(P, R, CT, Ctok)
    P.barrier()
    imasks = []
    for j in range(4):
        m = (np.arange(512)[None, :] == (j * 128 + np.arange(128))[:, None])
        t = R.cb.alloc(QC)
        P.gdma(t, P.const(m.astype(np.float32), f"im{j}")[:, :])
        imasks.append(t)
    R.xb.push(); R.rw.push(); R.fb.push()
    dscr = R.fb.alloc(QC)
    dcol = R.fb.alloc(1)
    diagrow = R.rw.row_alloc(QC)
    diagA_bc = R.fb.alloc(QC)
    dvt = [R.fb.alloc(QC), R.fb.alloc(QC)]
    tscr = R.fb.alloc(QC)
    ob = [R.xb.alloc(QC), R.xb.alloc(QC)]
    ost = [P.n("ds"), P.n("ds")]
    B3 = R.pb[3]

    def post_A(qc, hooks):
        for j in range(4):
            kbj = qc * 4 + j
            pcol = hooks["pt_buf"][0:128, kbj * QC:(kbj + 1) * QC]
            P.tt(dscr, pcol, imasks[j], ALU.mult, waits=[("mm", P.n("mm"))])
            P.vred(dcol, dscr, ALU.add, inc=("ve", 1))
            mmt = P.trans(B3[0:1, j * 128:(j + 1) * 128], dcol, R.identf,
                          waits=[("ve", P.n("ve"))] + (bank_waits(R, B3) if j == 0 else []),
                          inc=("mm", 1))
        P.tt(diagrow, B3[0:1, 0:512], hooks["rn_f32"], ALU.mult,
             waits=[("mm", mmt)])
        vv = P.vcopy(diagrow, diagrow, inc=("ve", 1))
        R.bank_set(B3, "ve", vv)
        mmb = P.mm(B3[0:128, 0:512], R.onesrow_f, diagrow, True, True,
                   waits=[("ve", vv)], inc=("mm", 1))
        vv = P.vcopy(diagA_bc, B3[0:128, 0:512], waits=[("mm", mmb)], inc=("ve", 1))
        R.bank_set(B3, "ve", vv)
        hooks["diag_ve"] = vv

    dvw = [0, 0]

    def evac_out(P_, R_, out_d, cc, qc, bank, mma, rb_sb, hooks):
        i = (qc * 8 + cc) % 2
        dv = P.gdma(dvt[i], dvnT[cc * 128:(cc + 1) * 128, qc * QC:(qc + 1) * QC],
                    waits=[("ve", dvw[i])] if dvw[i] else [])
        w = [("mm", mma), ("ve", hooks["rb_ve"]), ("dg", dv), ("ds", ost[i]),
             ("ve", hooks["diag_ve"])]
        P.tt(tscr, bank[0:128, 0:512], rb_sb, ALU.mult, waits=w)
        P.tt(dvt[i], dvt[i], diagA_bc, ALU.mult)
        vv = P.tt(ob[i], tscr, dvt[i], ALU.subtract, inc=("ve", 1))
        dvw[i] = vv
        ost[i] = P.sdma(att[cc * 128:(cc + 1) * 128, qc * QC:(qc + 1) * QC], ob[i],
                        waits=[("ve", vv)])
        return ("ve", vv)

    attn1024_pass(P, R, qkv[0:D, :], qkv[D:2 * D, :], Ctok, att, causal=True,
                  post_A=post_A, evac_out=evac_out)
    R.xb.pop(); R.rw.pop(); R.fb.pop()
    P.barrier()
    residual_add_pass(P, R, xin, att, xout)
    P.barrier()




def small_mlp_col(P, R, cols_in, nfi, nfo, w_dram, b2d, func, out_cols, dvw=0,
                  fo_size=128):
    fs = fo_size
    R.fb.push()
    wf = R.fb.alloc(nfi * nfo * fs)
    dgw = 0
    for fi in range(nfi):
        dgw = P.sdma(wf[0:128, fi * nfo * fs:(fi + 1) * nfo * fs],
                     w_dram[fi * 128:(fi + 1) * 128, 0:nfo * fs],
                     waits=[("mm", P.n("mm"))] if fi == 0 else [])
    def wt(fi, fo):
        return wf[0:128, fi * nfo * fs + fo * fs: fi * nfo * fs + (fo + 1) * fs]
    bt, dvb = stage_colvec(P, R, b2d, nfo)
    B = R.pb[3]
    for fo in range(nfo):
        mmv = None
        for fi in range(nfi):
            w = []
            if fi == 0:
                w = [("ds", dgw), ("dg", max(dvb, dvw)), ("ve", P.n("ve"))] + bank_waits(R, B)
            mmv = P.mm(B[0:fs, fo:fo + 1], wt(fi, fo), cols_in[0:128, fi:fi + 1],
                       fi == 0, fi == nfi - 1, waits=w, inc=("mm", 1))
        acv = P.act(out_cols[0:fs, fo:fo + 1], B[0:fs, fo:fo + 1], func,
                    bias=bt[0:fs, fo:fo + 1], waits=[("mm", mmv)], inc=("ac", 1))
        R.bank_set(B, "ac", acv)
    R.fb.pop()
    return P.n("ac")


def emit_l2_ffn(P, R, xin, xout, d, tag):
    h = P.dram_t("s_h", [D, T], BF16)
    hf = P.dram_t("s_hf", [D, T], F32)
    of = [P.dram_t(f"s_of{i}_{tag}", [D, T], F32) for i in range(2)]
    obf = [P.dram_t(f"s_ob{i}_{tag}", [D, T], BF16) for i in range(2)]
    c1 = P.dram_t("s_c1", [D, T], BF16)
    c2 = P.dram_t("s_c2", [512, T], BF16)
    c3 = P.dram_t("s_c3", [256, T], BF16)
    ln_pass(P, R, xin, h, h1_dram=hf)
    P.barrier()
    R.xb.push(); R.rw.push(); R.fb.push()
    gbar = R.rw.alloc(8)
    hch = R.xb.alloc(T)
    for c in range(NCH):
        dv = P.sdma(hch, h[c * 128:(c + 1) * 128, :], waits=[("ve", P.n("ve"))])
        P.vred(gbar[0:128, c:c + 1], hch, ALU.add, waits=[("ds", dv)], inc=("ve", 1))
    vv = P.ts(gbar, gbar, 1.0 / T, ALU.mult, inc=("ve", 1))
    g1 = R.rw.alloc(4)
    g2 = R.rw.alloc(2)
    gfc = R.rw.alloc(1)
    R.wb.push()
    small_mlp_col(P, R, gbar, 8, 4, d["gc_w1"], d["gc_b1_2d"], AF.Gelu, g1)
    small_mlp_col(P, R, g1, 4, 2, d["gc_w2"], d["gc_b2_2d"], AF.Gelu, g2)
    acg = small_mlp_col(P, R, g2, 2, 1, d["gc_w3"], d["gc_b3_2d"], AF.Sigmoid, gfc, fo_size=1)
    R.wb.pop()
    ar = R.rw.alloc(1)
    a0 = R.rw.alloc(1)
    a1 = R.rw.alloc(1)
    gf07 = R.rw.alloc(1)
    dva = P.gdma(ar[0:1, 0:1], d["adaptive_range"][0:1])
    P.ts(gf07[0:1, :], gfc[0:1, 0:1], 0.7, ALU.mult, waits=[("ac", acg), ("dg", dva)])
    P.ts(a0[0:1, :], ar[0:1, 0:1], -1.0, ALU.mult, s2=0.1, op1=ALU.add)
    vv = P.ts(a1[0:1, :], ar[0:1, 0:1], 2.0, ALU.mult, inc=("ve", 1))
    P.barrier()
    twist_src = [4, 5, 6, 7, 0, 1, 2, 3]
    twist_sgn = [1, 1, -1, -1, -1, -1, 1, 1]
    tf_row = R.rw.row_alloc(QC)
    coup = R.rw.row_alloc(QC)
    cb = R.fb.alloc(QC)
    t_a = R.fb.alloc(QC)
    t_b = R.fb.alloc(QC)
    t_o = R.fb.alloc(QC)
    obb = R.xb.alloc(QC)
    Bb = R.pb[4]
    for cyc in range(3):
        src_f = hf if cyc == 0 else of[(cyc + 1) % 2]
        src_b = h if cyc == 0 else obf[(cyc + 1) % 2]
        dst_f = of[cyc % 2]
        dst_b = obf[cyc % 2]
        bt1, dv1 = stage_colvec(P, R, d["cn_b1_2d"], 8)
        ev = mk_evac_store(P, R, c1, func=AF.Gelu, bias_tile=bt1, dvb=dv1)
        proj_pass(P, R, d["cn_w1"], src_b, nfi=8, nfo=8, evac=ev)
        bt2, dv2 = stage_colvec(P, R, d["cn_b2_2d"], 4)
        ev = mk_evac_store(P, R, c2, func=AF.Gelu, bias_tile=bt2, dvb=dv2)
        proj_pass(P, R, d["cn_w2"], c1, nfi=8, nfo=4, evac=ev)
        bt3, dv3 = stage_colvec(P, R, d["cn_b3_2d"], 2)
        ev = mk_evac_store(P, R, c3, func=AF.Gelu, bias_tile=bt3, dvb=dv3)
        proj_pass(P, R, d["cn_w3"], c2, nfi=4, nfo=2, evac=ev)
        P.barrier()
        R.wb.push()
        wt4, dg4 = load_weight_lhsT(P, R, d["cn_w4"], 2, 1, fo_size=1)
        b4, dv4 = stage_colvec(P, R, d["cn_b4_2d"], 1)
        c3t = [R.xb.alloc(QC), R.xb.alloc(QC)]
        BL = R.pb[5]
        stv = [0, 0]
        for qc in range(NQC):
            dv = 0
            for fi in range(2):
                dv = P.sdma(c3t[fi], c3[fi * 128:(fi + 1) * 128, qc * QC:(qc + 1) * QC],
                            waits=[("mm", P.n("mm"))])
            mml = None
            for fi in range(2):
                w = [("ds", dv), ("dg", max(dg4, dv4))] + (bank_waits(R, BL) if fi == 0 else [])
                mml = P.mm(BL[0:1, :], wt4(fi, 0), c3t[fi], fi == 0, fi == 1,
                           waits=w, inc=("mm", 1))
            acv = P.act(tf_row, BL[0:1, 0:512], AF.Sigmoid, bias=b4[0:1, 0:1],
                        waits=[("mm", mml)], inc=("ac", 1))
            P.ts(coup, tf_row, 0.3, ALU.mult, s2=gf07[0:1, 0:1], op1=ALU.add,
                 waits=[("ac", acv)])
            vv = P.ts(coup, coup, a1[0:1, 0:1], ALU.mult, s2=a0[0:1, 0:1],
                      op1=ALU.add, inc=("ve", 1))
            R.bank_set(BL, "ve", vv)
            mmb = P.mm(Bb[0:128, 0:512], R.onesrow_f, coup, True, True,
                       waits=[("ve", vv)] + bank_waits(R, Bb), inc=("mm", 1))
            vcb = P.vcopy(cb, Bb[0:128, 0:512], waits=[("mm", mmb)], inc=("ve", 1))
            R.bank_set(Bb, "ve", vcb)
            for c in range(NCH):
                sgn = twist_sgn[c]
                sc = twist_src[c]
                dva_ = P.sdma(t_a, src_f[c * 128:(c + 1) * 128, qc * QC:(qc + 1) * QC],
                              waits=[("ve", P.n("ve"))])
                dvb_ = P.sdma(t_b, src_f[sc * 128:(sc + 1) * 128, qc * QC:(qc + 1) * QC])
                P.tt(t_o, t_b, cb, ALU.mult, waits=[("ds", max(dvb_, stv[0], stv[1]))])
                vv = P.tt(t_o, t_a, t_o, ALU.add if sgn > 0 else ALU.subtract,
                          inc=("ve", 1))
                stv[0] = P.sdma(dst_f[c * 128:(c + 1) * 128, qc * QC:(qc + 1) * QC],
                                t_o, waits=[("ve", vv)])
                if cyc < 2:
                    acb = P.act(obb, t_o, AF.Copy,
                                waits=[("ve", vv), ("ds", stv[1])], inc=("ac", 1))
                    stv[1] = P.sdma(dst_b[c * 128:(c + 1) * 128, qc * QC:(qc + 1) * QC],
                                    obb, waits=[("ac", acb)])
        R.wb.pop()
        P.barrier()
    R.xb.pop(); R.rw.pop(); R.fb.pop()
    residual_add_pass(P, R, xin, of[0], xout, add_bf16=False)
    P.barrier()




def emit_l3_attn_b(P, R, xin, xout, d):
    import math as _m
    h = P.dram_t("s_h", [D, T], BF16)
    qkvu = P.dram_t("s_qkvu", [4 * D, T], BF16)
    oT = P.dram_t("s_ot", [D, T], BF16)
    on2 = P.dram_t("s_on2", [D, T], BF16)
    ln_pass(P, R, xin, h)
    P.barrier()
    R.rw.push(); R.xb.push(); R.fb.push()
    bq, dvb = stage_colvec(P, R, d["bqkvu2d"], 32)

    def ffn_(fo):
        return AF.Silu if fo < 16 else None
    ev = mk_evac_store(P, R, qkvu, bias_tile=bq, dvb=dvb, func_fn=ffn_)
    proj_pass(P, R, d["WqkvuT"], h, nfi=8, nfo=32, evac=ev)
    P.barrier()
    TWO_PI = 2.0 * _m.pi
    n_bc = R.fb.alloc(T, parts=64)
    P.ins("gpsimd", lambda e, a=n_bc: e.iota(a, pattern=[[1, T]], base=0,
                                             channel_multiplier=0,
                                             allow_small_or_imprecise_dtypes=True),
          inc=("gp", 1))
    th_col = R.rw.alloc(1)
    th2_col = R.rw.alloc(1)
    fr1 = R.fb.alloc(QC, parts=64)
    fr2 = R.fb.alloc(QC, parts=64)
    cos_t = R.xb.alloc(T, parts=64)
    sin_t = R.xb.alloc(T, parts=64)
    qh = R.xb.alloc(T, parts=64)
    kh = R.xb.alloc(T, parts=64)
    vh = R.xb.alloc(T, parts=64)
    qlc = R.xb.alloc(QC, parts=64)
    qls = R.xb.alloc(QC, parts=64)
    klc = R.xb.alloc(128, parts=64)
    kls = R.xb.alloc(128, parts=64)
    ktok = R.xb.alloc(128)
    vtok = R.xb.alloc(64)
    kvA = R.xb.alloc(64, parts=64)
    kvB = R.xb.alloc(64, parts=64)
    ob = [R.xb.alloc(QC, parts=64) for _ in range(3)]
    ech = R.rw.row_alloc(1)
    ost = [0, 0, 0]
    Bkv = R.pb[4]
    Bo = [R.pb[0], R.pb[1], R.pb[2]]
    ibt = R.ib.h[0:64, 0:QC]
    for hh in range(16):
        dth = P.gdma(th_col[0:64, 0:1],
                     d["theta"][hh:hh + 1, 0:1, 0:64].rearrange("a b c -> (a c) b"),
                     waits=[("ve", P.n("ve"))])
        vv0 = P.ts(th2_col[0:64, 0:1], th_col[0:64, 0:1], 1.0 / TWO_PI, ALU.mult,
                   waits=[("dg", dth), ("gp", 1), ("ac", P.n("ac"))], inc=("ve", 1))
        for j in range(NQC):
            sl = slice(j * QC, (j + 1) * QC)
            for (dst, off) in ((sin_t, 0.0), (cos_t, _m.pi / 2)):
                P.ts(fr1, n_bc[0:64, sl], th2_col[0:64, 0:1],
                     ALU.mult, s2=off / TWO_PI, op1=ALU.add,
                     waits=[("ve", vv0), ("ac", P.n("ac"))])
                P.vcopy(ibt, fr1)
                P.vcopy(fr2, ibt)
                P.ts(fr1, fr1, TWO_PI, ALU.mult)
                P.stt(fr1, fr2, -TWO_PI, ALU.mult, fr1, ALU.add)
                P.ts(fr2, fr1, _m.pi, ALU.is_gt)
                P.stt(fr1, fr2, -TWO_PI, ALU.mult, fr1, ALU.add)
                P.ts(fr2, fr1, -_m.pi, ALU.is_lt)
                vj = P.stt(fr1, fr2, TWO_PI, ALU.mult, fr1, ALU.add, inc=("ve", 1))
                P.act(dst[0:64, sl], fr1, AF.Sin, waits=[("ve", vj)], inc=("ac", 1))
        dq = P.sdma(qh, qkvu[hh * 64:(hh + 1) * 64, :], waits=[("mm", P.n("mm")),
                                                               ("ve", P.n("ve"))])
        dk = P.sdma(kh, qkvu[D + hh * 64:D + (hh + 1) * 64, :])
        dv = P.sdma(vh, qkvu[2 * D + hh * 64:2 * D + (hh + 1) * 64, :])
        w = [("ds", dv), ("ac", P.n("ac"))]
        mkv = None
        vv2 = 0
        for tb in range(32):
            tsl = slice(tb * 128, (tb + 1) * 128)
            P.tt(klc, kh[0:64, tsl], cos_t[0:64, tsl], ALU.mult,
                 waits=w + ([("mm", mkv)] if mkv else []))
            vv = P.tt(kls, kh[0:64, tsl], sin_t[0:64, tsl], ALU.mult, inc=("ve", 1))
            for (src_t, ssl, c0) in ((klc, slice(0, 128), 0),
                                     (kls, slice(0, 128), 64),
                                     (vh, tsl, -1)):
                ptb = R.pt[(tb + (1 if c0 == 64 else 0)) % 2]
                mmv = P.trans(ptb[0:128, 0:64], src_t[0:64, ssl],
                              R.identb[0:64, 0:64],
                              waits=[("ve", vv)] + bank_waits(R, ptb), inc=("mm", 1))
                tgt = vtok[0:128, 0:64] if c0 < 0 else ktok[0:128, c0:c0 + 64]
                P.vcopy(tgt, ptb[0:128, 0:64], waits=[("mm", mmv)])
                vv2 = P.vcopy(ech, tgt[0:1, 0:1], inc=("ve", 1))
                R.bank_set(ptb, "ve", vv2)
            w2 = [("ve", vv2)] + (bank_waits(R, Bkv) if tb == 0 else [])
            mkv = P.mm(Bkv[0:128, 0:64], ktok, vtok, tb == 0, tb == 31,
                       waits=w2, inc=("mm", 1))
        P.vcopy(kvA, Bkv[0:64, 0:64], waits=[("mm", mkv)])
        P.vcopy(kvB, Bkv[64:128, 0:64])
        vkv = P.vcopy(ech, kvA[0:1, 0:1], inc=("ve", 1))
        R.bank_set(Bkv, "ve", vkv)
        for qc in range(NQC):
            sl = slice(qc * QC, (qc + 1) * QC)
            bo = Bo[qc % 3]
            P.tt(qlc, qh[0:64, sl], cos_t[0:64, sl], ALU.mult,
                 waits=[("mm", P.n("mm"))])
            vq = P.tt(qls, qh[0:64, sl], sin_t[0:64, sl], ALU.mult, inc=("ve", 1))
            w3 = [("ve", max(vkv, vq))] + bank_waits(R, bo)
            P.mm(bo[0:64, 0:512], kvA, qlc[0:64, 0:512], True, False, waits=w3)
            mo = P.mm(bo[0:64, 0:512], kvB, qls[0:64, 0:512], False, True, inc=("mm", 1))
            i = qc % 3
            vo = P.vcopy(ob[i], bo[0:64, 0:512],
                         waits=[("mm", mo)] + ([("ds", ost[i])] if ost[i] else []),
                         inc=("ve", 1))
            R.bank_set(bo, "ve", vo)
            ost[i] = P.sdma(oT[hh * 64:(hh + 1) * 64, sl], ob[i][0:64, :],
                            waits=[("ve", vo)])
    R.rw.pop(); R.xb.pop(); R.fb.pop()
    P.barrier()
    vdbg = P.dram_t("s_vdbg", [1, T], F32)
    ln_pass(P, R, oT, on2, gamma_beta=(d["ln_w"], d["ln_b"]),
            mult_dram=qkvu[3 * D:4 * D, :], var_dram=vdbg)
    P.barrier()
    R.rw.push(); R.fb.push()
    bo2, dvo = stage_colvec(P, R, d["bout2d"], 8)
    ev = mk_evac_residual(P, R, xin, xout, bias_tile=bo2, dvb=dvo)
    proj_pass(P, R, d["WoutT"], on2, nfi=8, nfo=8, evac=ev)
    R.rw.pop(); R.fb.pop()
    P.barrier()




def rope_pass_l2b(P, R, rot_dram, S=4096):
    DRH = 8
    inv_freq = 1.0 / 10000.0 ** (np.arange(0, DRH, 2, dtype=np.float32) / DRH)
    t = np.arange(S, dtype=np.float32) / 40.0
    freqs = np.outer(t, inv_freq)
    cosr = np.cos(freqs).T
    sinr = np.sin(freqs).T
    cos64 = np.ascontiguousarray(np.tile(cosr, (16, 1)))
    sin64 = np.ascontiguousarray(np.tile(sinr, (16, 1)))
    R.xb.push()
    ct = R.xb.alloc(T, parts=64)
    st_ = R.xb.alloc(T, parts=64)
    x1 = R.xb.alloc(T, parts=64)
    x2 = R.xb.alloc(T, parts=64)
    n1 = R.xb.alloc(T, parts=64)
    n2 = R.xb.alloc(T, parts=64)
    dc = P.gdma(ct, P.const(cos64, "l2c")[:, :])
    dc = P.gdma(st_, P.const(sin64, "l2s")[:, :])
    dv = P.sdma(x1, rot_dram[0:64, :], waits=[("ve", P.n("ve"))])
    dv = P.sdma(x2, rot_dram[128:192, :])
    w = [("ds", dv), ("dg", dc)]
    P.tt(n1, x2, st_, ALU.mult, waits=w)
    P.tt(n2, x1, ct, ALU.mult)
    P.tt(n1, n2, n1, ALU.subtract)
    P.tt(n2, x2, ct, ALU.mult)
    P.tt(x1, x1, st_, ALU.mult)
    vv = P.tt(n2, n2, x1, ALU.add, inc=("ve", 1))
    P.sdma(rot_dram[0:64, :], n1, waits=[("ve", vv)])
    P.sdma(rot_dram[128:192, :], n2)
    R.xb.pop()


def l2b_perm():
    perm = []
    for blk in (range(0, 4), range(8, 12), range(4, 8), range(12, 16)):
        for i in blk:
            pass
    perm = []
    for i in range(0, 4):
        for h in range(16):
            pass
    perm = []
    for blk in ([0, 1, 2, 3], [8, 9, 10, 11], [4, 5, 6, 7], [12, 13, 14, 15]):
        for h in range(16):
            for i in blk:
                perm.append(h * 16 + i)
    return np.array(perm)


def l2b_rows(base_t, rot_t):
    def rows(h):
        return [(base_t, h * 48, 48),
                (rot_t, h * 4, 4),
                (rot_t, 128 + h * 4, 4),
                (rot_t, 64 + h * 4, 4),
                (rot_t, 192 + h * 4, 4)]
    return rows


def emit_l2_attn_b(P, R, xin, xout, d):
    h = P.dram_t("s_h", [D, T], BF16)
    ckv = P.dram_t("s_ckv", [128, T], BF16)
    cq = P.dram_t("s_cq", [128, T], BF16)
    kb = P.dram_t("s_kb", [768, T], BF16)
    vT = P.dram_t("s_v", [D, T], BF16)
    qb = P.dram_t("s_qb", [768, T], BF16)
    qr = P.dram_t("s_qr", [256, T], BF16)
    kr = P.dram_t("s_kr", [256, T], BF16)
    att = P.dram_t("s_att", [D, T], BF16)
    ln_pass(P, R, xin, h)
    P.barrier()
    R.rw.push(); R.xb.push()
    b1, dv1 = stage_colvec(P, R, d["b_dkv2d"], 1)
    ev = mk_evac_store(P, R, ckv, bias_tile=b1, dvb=dv1)
    proj_pass(P, R, d["W_dkv"], h, nfi=8, nfo=1, evac=ev)
    b2, dv2 = stage_colvec(P, R, d["b_dq2d"], 1)
    ev = mk_evac_store(P, R, cq, bias_tile=b2, dvb=dv2)
    proj_pass(P, R, d["W_dq"], h, nfi=8, nfo=1, evac=ev)
    P.barrier()
    b3, dv3 = stage_colvec(P, R, d["b_uk2d"], 6)
    ev = mk_evac_store(P, R, kb, bias_tile=b3, dvb=dv3)
    proj_pass(P, R, d["W_uk"], ckv, nfi=1, nfo=6, evac=ev)
    b4, dv4 = stage_colvec(P, R, d["b_uv2d"], 8)
    ev = mk_evac_store(P, R, vT, bias_tile=b4, dvb=dv4)
    proj_pass(P, R, d["W_uv"], ckv, nfi=1, nfo=8, evac=ev)
    b5, dv5 = stage_colvec(P, R, d["b_uq2d"], 6)
    ev = mk_evac_store(P, R, qb, bias_tile=b5, dvb=dv5)
    proj_pass(P, R, d["W_uq"], cq, nfi=1, nfo=6, evac=ev)
    b6, dv6 = stage_colvec(P, R, d["b_qr2d"], 2)
    ev = mk_evac_store(P, R, qr, bias_tile=b6, dvb=dv6)
    proj_pass(P, R, d["W_qrp"], cq, nfi=1, nfo=2, evac=ev)
    b7, dv7 = stage_colvec(P, R, d["b_kr2d"], 2)
    ev = mk_evac_store(P, R, kr, bias_tile=b7, dvb=dv7)
    proj_pass(P, R, d["W_krp"], h, nfi=8, nfo=2, evac=ev)
    P.barrier()
    R.xb.pop(); R.rw.pop()
    rope_pass_l2b(P, R, qr)
    rope_pass_l2b(P, R, kr)
    P.barrier()
    mha64_pass(P, R, qb, kb, vT, att, 16, 64, False, 0.125,
               q_rows=l2b_rows(qb, qr), k_rows=l2b_rows(kb, kr))
    P.barrier()
    R.rw.push(); R.fb.push()
    bo, dvo = stage_colvec(P, R, d["b_o2d"], 8)
    ev = mk_evac_residual(P, R, xin, xout, bias_tile=bo, dvb=dvo)
    proj_pass(P, R, d["W_o"], att, nfi=8, nfo=8, evac=ev)
    R.rw.pop(); R.fb.pop()
    P.barrier()




def softmax16(P, R, w_dram, out16):
    wt = R.fb.alloc(16)
    mx = R.rw.alloc(1)
    sm = R.rw.alloc(1)
    dv = P.sdma(wt[0:16, 0:16], w_dram[:, :], waits=[("ve", P.n("ve")),
                                                     ("ac", P.n("ac"))])
    P.vred(mx[0:16, 0:1], wt[0:16, 0:16], ALU.max, waits=[("ds", dv)])
    vv = P.ts(mx[0:16, 0:1], mx[0:16, 0:1], -1.0, ALU.mult, inc=("ve", 1))
    acv = P.act(out16[0:16, 0:16], wt[0:16, 0:16], AF.Exp, bias=mx[0:16, 0:1],
                waits=[("ve", vv)], inc=("ac", 1))
    P.vred(sm[0:16, 0:1], out16[0:16, 0:16], ALU.add, waits=[("ac", acv)])
    P.vrecip(sm[0:16, 0:1], sm[0:16, 0:1])
    return P.ts(out16[0:16, 0:16], out16[0:16, 0:16], sm[0:16, 0:1], ALU.mult,
                inc=("ve", 1))


def emit_l2_attn_a(P, R, xin, xout, d):
    h = P.dram_t("s_h", [D, T], BF16)
    att = P.dram_t("s_att", [D, T], BF16)
    wkd = P.dram_t("s_wkd", [D, D], BF16)
    wqd = P.dram_t("s_wqd", [D, D], BF16)
    wvd = P.dram_t("s_wvd", [D, D], BF16)
    qT = P.dram_t("s_q", [D, T], BF16)
    kT = P.dram_t("s_k", [D, T], BF16)
    vT = P.dram_t("s_v", [D, T], BF16)
    pdbg = P.dram_t("s_probs", [3, 16, 16], F32)
    ln_pass(P, R, xin, h)
    P.barrier()
    R.rw.push(); R.fb.push(); R.xb.push()
    i2 = R.cb.alloc(64)
    ey = np.eye(64, dtype=np.float32)
    P.gdma(i2, P.const(np.vstack([ey, ey]), "i2")[:, :])
    wps = R.xb.alloc(16 * 64)
    dwp = P.gdma(wps[0:64, :].rearrange("p (h b) -> p h b", b=64),
                 d["W_pred"].rearrange("h a b -> a h b"))
    dwp = P.gdma(wps[64:128, 0:1024].rearrange("p (h b) -> p h b", b=64),
                 d["W_pred"].rearrange("h a b -> a h b"))
    pr = R.fb.alloc(16)
    pbc = R.fb.alloc(256)
    wch = R.xb.alloc(D)
    Bb = R.pb[5]
    stv = 0
    for (wi, wd, src_ap) in (("W_K", wkd, None), ("W_Q", wqd, wps), ("W_V", wvd, None)):
        vv = softmax16(P, R, d[wi], pr)
        P.sdma(pdbg[("W_K", "W_Q", "W_V").index(wi), :, :], pr[0:16, 0:16],
               waits=[("ve", vv)])
        prow = P.dram_t(f"s_prow_{wi}", [16, 16], F32)
        dstv = P.sdma(prow[:, :], pr[0:16, 0:16], waits=[("ve", vv)])
        pfl = R.rw.row_alloc(256)
        dv = P.sdma(pfl[0:1, 0:256], prow.rearrange("a b -> (a b)"),
                    waits=[("ds", dstv)])
        mmb = P.mm(Bb[0:128, 0:256], R.onesrow_f, pfl[0:1, 0:256], True, True,
                   waits=[("ds", dv)] + bank_waits(R, Bb), inc=("mm", 1))
        vcb = P.vcopy(pbc[0:128, 0:256], Bb[0:128, 0:256], waits=[("mm", mmb)],
                      inc=("ve", 1))
        R.bank_set(Bb, "ve", vcb)
        for c in range(NCH):
            vz = P.vmemset(wch, 0.0, waits=[("ds", stv), ("dg", dwp)], inc=("ve", 1))
            for vsub in range(2):
                vglob = c * 2 + vsub
                base = vsub * 64
                for hh in range(16):
                    src = i2[base:base + 64, 0:64] if src_ap is None else                         src_ap[base:base + 64, hh * 64:(hh + 1) * 64]
                    P.ts(wch[base:base + 64, hh * 64:(hh + 1) * 64], src,
                         pbc[base:base + 64, hh * 16 + vglob:hh * 16 + vglob + 1],
                         ALU.mult, waits=[("ve", vcb)])
            vv2 = P.ins("vector", lambda e, a=wch: e.tensor_copy(a[0:1, 0:1], a[0:1, 0:1]),
                        inc=("ve", 1))
            stv = P.sdma(wd[c * 128:(c + 1) * 128, :], wch, waits=[("ve", vv2)])
    P.barrier()
    R.rw.pop(); R.fb.pop(); R.xb.pop()
    R.xb.push()
    for (wd, dst) in ((wkd, kT), (wqd, qT), (wvd, vT)):
        ev = mk_evac_store(P, R, dst)
        proj_pass(P, R, wd, h, nfi=8, nfo=8, evac=ev)
    P.barrier()
    R.xb.pop()
    mha64_pass(P, R, qT, kT, vT, att, 16, 64, True, 0.125)
    P.barrier()
    residual_add_pass(P, R, xin, att, xout)
    P.barrier()




def emit_l3_ffn(P, R, xin, xout, d, tag):
    h = P.dram_t("s_h", [D, T], BF16)
    xs = P.dram_t("s_xs", [D, T], BF16)
    hsh = P.dram_t("s_hsh", [2048, T], BF16)
    xmid = P.dram_t(f"s_xmid_{tag}", [D, T], F32)
    racc = P.dram_t(f"s_racc_{tag}", [D, T], F32)
    msk_d = P.dram_t("s_msk", [8, T], BF16)
    hf = P.dram_t("s_hf", [D, T], F32)
    ln_pass(P, R, xin, h, h1_dram=hf)
    P.barrier()
    R.rw.push(); R.fb.push(); R.xb.push()
    lg = R.rw.alloc(T, parts=8)
    wgf = R.fb.alloc(64)
    dgw = 0
    for c in range(NCH):
        dgw = P.sdma(wgf[0:128, c * 8:(c + 1) * 8],
                     d["gate_wT"][c * 128:(c + 1) * 128, :],
                     waits=[("mm", P.n("mm"))] if c == 0 else [])
    xtf = [R.fb.alloc(QC) for _ in range(NCH)]
    Bq = [R.pb[0], R.pb[1], R.pb[2]]
    for qc in range(NQC):
        dv = 0
        for c in range(NCH):
            dv = P.sdma(xtf[c], hf[c * 128:(c + 1) * 128, qc * QC:(qc + 1) * QC],
                        waits=[("mm", P.n("mm"))])
        B = Bq[qc % 3]
        mmv = None
        for c in range(NCH):
            w = ([("ds", max(dv, dgw))] + bank_waits(R, B)) if c == 0 else []
            mmv = P.mm(B[0:8, 0:512], wgf[0:128, c * 8:(c + 1) * 8], xtf[c],
                       c == 0, c == NCH - 1, waits=w, inc=("mm", 1))
        vv = P.vcopy(lg[0:8, qc * QC:(qc + 1) * QC], B[0:8, 0:512],
                     waits=[("mm", mmv)], inc=("ve", 1))
        R.bank_set(B, "ve", vv)
    msk_sb = R.xb.alloc(T, parts=8)
    sc_row = R.rw.row_alloc(T)
    lgt = R.fb.alloc(8)
    mx = R.fb.alloc(1)
    mskt = R.xb.alloc(8)
    scb = R.xb.alloc(1)
    ech = R.rw.row_alloc(1)
    B1, B2 = R.pt[0], R.pt[1]
    Bf = R.pb[3]
    for tb in range(32):
        mmt = P.trans(Bf[0:128, 0:8], lg[0:8, tb * 128:(tb + 1) * 128],
                      R.identf[0:8, 0:8],
                      waits=[("ve", P.n("ve"))] + bank_waits(R, Bf), inc=("mm", 1))
        P.vcopy(lgt, Bf[0:128, 0:8], waits=[("mm", mmt)])
        vv = P.vcopy(ech, lgt[0:1, 0:1], inc=("ve", 1))
        R.bank_set(Bf, "ve", vv)
        P.vred(mx, lgt, ALU.max)
        vv = P.ts(mskt, lgt, mx[0:128, 0:1], ALU.is_ge, inc=("ve", 1))
        acs = P.act(scb, mx, AF.Sigmoid, waits=[("ve", vv)], inc=("ac", 1))
        mm2 = P.trans(B1[0:8, 0:128], mskt, R.identb[0:128, 0:128],
                      waits=[("ve", vv)] + bank_waits(R, B1), inc=("mm", 1))
        P.vcopy(msk_sb[0:8, tb * 128:(tb + 1) * 128], B1[0:8, 0:128],
                waits=[("mm", mm2)])
        vv = P.vcopy(ech, msk_sb[0:1, tb * 128:tb * 128 + 1], inc=("ve", 1))
        R.bank_set(B1, "ve", vv)
        mm3 = P.trans(B2[0:1, 0:128], scb, R.identb[0:128, 0:128],
                      waits=[("ac", acs)] + bank_waits(R, B2), inc=("mm", 1))
        P.vcopy(sc_row[0:1, tb * 128:(tb + 1) * 128], B2[0:1, 0:128],
                waits=[("mm", mm3)])
        vv = P.vcopy(ech, sc_row[0:1, tb * 128:tb * 128 + 1], inc=("ve", 1))
        R.bank_set(B2, "ve", vv)
    dms = P.sdma(msk_d[:, :], msk_sb[0:8, :], waits=[("ve", P.n("ve"))])
    Bb = R.pb[4]
    xsb = R.xb.alloc(QC)
    xtb = R.xb.alloc(QC)
    scb2 = R.fb.alloc(QC)
    stv = 0
    for qc in range(NQC):
        mmb = P.mm(Bb[0:128, 0:512], R.onesrow_f, sc_row[0:1, qc * QC:(qc + 1) * QC],
                   True, True, waits=[("ve", P.n("ve"))] + bank_waits(R, Bb),
                   inc=("mm", 1))
        vcb = P.vcopy(scb2, Bb[0:128, 0:512], waits=[("mm", mmb)], inc=("ve", 1))
        R.bank_set(Bb, "ve", vcb)
        for c in range(NCH):
            dv = P.sdma(xtb, h[c * 128:(c + 1) * 128, qc * QC:(qc + 1) * QC],
                        waits=[("ve", P.n("ve"))])
            vv = P.tt(xsb, xtb, scb2, ALU.mult, waits=[("ds", max(dv, stv))],
                      inc=("ve", 1))
            stv = P.sdma(xs[c * 128:(c + 1) * 128, qc * QC:(qc + 1) * QC], xsb,
                         waits=[("ve", vv)])
    P.barrier()
    R.xb.pop(); R.fb.pop(); R.rw.pop()
    R.xb.push()
    sg_t = P.dram_t("s_sg", [2048, T], BF16)
    ev = mk_evac_store(P, R, sg_t, func=AF.Silu)
    proj_pass(P, R, d["shared_gateT"], h, nfi=8, nfo=16, evac=ev)
    ob2 = [R.xb.alloc(QC), R.xb.alloc(QC)]
    sgt2 = [R.xb.alloc(QC), R.xb.alloc(QC)]
    st2 = [P.n("ds"), P.n("ds")]
    ld2 = [0, 0]
    ic = [0]

    def evac_up(fo, qc, bank, mmv):
        i = ic[0] % 2
        ic[0] += 1
        dv = P.sdma(sgt2[i], sg_t[fo * 128:(fo + 1) * 128, qc * QC:(qc + 1) * QC],
                    waits=[("ve", ld2[i])] if ld2[i] else [])
        vv = P.tt(ob2[i], bank[0:128, 0:512], sgt2[i], ALU.mult,
                  waits=[("mm", mmv), ("ds", max(dv, st2[i]))], inc=("ve", 1))
        ld2[i] = vv
        st2[i] = P.sdma(hsh[fo * 128:(fo + 1) * 128, qc * QC:(qc + 1) * QC],
                        ob2[i], waits=[("ve", vv)])
        return ("ve", vv)
    proj_pass(P, R, d["shared_upT"], h, nfi=8, nfo=16, evac=evac_up)
    P.barrier()
    R.fb.push()
    ev = mk_evac_residual(P, R, xin, xmid)
    proj_pass(P, R, d["shared_downT"], hsh, nfi=16, nfo=8, evac=ev)
    R.fb.pop()
    P.barrier()
    R.xb.pop()
    R.xb.push(); R.fb.push(); R.rw.push()
    xsp = [R.xb.alloc(QC) for _ in range(NCH)]
    het = [R.xb.alloc(QC) for _ in range(8)]
    sgt = R.xb.alloc(QC)
    umt = R.xb.alloc(QC)
    mrow1 = R.xb.alloc(QC, parts=1)
    mbc = R.xb.alloc(QC)
    orow = [R.fb.alloc(QC), R.fb.alloc(QC)]
    ost = [0, 0]
    Bg = [R.pb[0], R.pb[1]]
    Bu = [R.pb[2], R.pb[3]]
    Bd = [R.pb[5], R.pb[4]]
    Bb2 = R.pb[4]
    prev_acc = 0
    for e in range(8):
        for half in range(2):
            R.wb.push()
            wgm, dg1 = load_weight_lhsT(P, R, d["r_gateT"][e], 8, 8, fo0=half * 8)
            wum, dg2 = load_weight_lhsT(P, R, d["r_upT"][e], 8, 8, fo0=half * 8)
            wdm, dg3 = load_weight_lhsT(
                P, R, d["r_downT"][e][half * 1024:(half + 1) * 1024, :], 8, 8)
            first_acc = True
            for qc in range(NQC):
                sl = slice(qc * QC, (qc + 1) * QC)
                dv = 0
                for c in range(NCH):
                    dv = P.sdma(xsp[c], xs[c * 128:(c + 1) * 128, sl],
                                waits=[("mm", P.n("mm"))] if c == 0 else [])
                dmr = P.sdma(mrow1, msk_d[e:e + 1, sl], waits=[("mm", P.n("mm"))])
                mmb = P.mm(Bb2[0:128, 0:512], R.onesrow_b, mrow1, True, True,
                           waits=[("ds", dmr)] + bank_waits(R, Bb2), inc=("mm", 1))
                vcb = P.vcopy(mbc, Bb2[0:128, 0:512], waits=[("mm", mmb)],
                              inc=("ve", 1))
                R.bank_set(Bb2, "ve", vcb)
                for fo in range(8):
                    bg = Bg[fo % 2]
                    bu = Bu[fo % 2]
                    mg = None
                    for fi in range(NCH):
                        w = ([("ds", dv), ("dg", dg3)] + bank_waits(R, bg)) if fi == 0 else []
                        mg = P.mm(bg[0:128, 0:512], wgm(fi, fo), xsp[fi],
                                  fi == 0, fi == NCH - 1, waits=w, inc=("mm", 1))
                    acg = P.act(sgt, bg[0:128, 0:512], AF.Silu,
                                waits=[("mm", mg), ("ve", P.n("ve"))], inc=("ac", 1))
                    R.bank_set(bg, "ac", acg)
                    mu_ = None
                    for fi in range(NCH):
                        w = (bank_waits(R, bu)) if fi == 0 else []
                        mu_ = P.mm(bu[0:128, 0:512], wum(fi, fo), xsp[fi],
                                   fi == 0, fi == NCH - 1, waits=w, inc=("mm", 1))
                    P.tt(umt, bu[0:128, 0:512], mbc, ALU.mult,
                         waits=[("mm", mu_), ("ve", vcb)])
                    vh = P.tt(het[fo], sgt, umt, ALU.mult,
                              waits=[("ac", acg)], inc=("ve", 1))
                    R.bank_set(bu, "ve", vh)
                for fo in range(8):
                    bd = Bd[fo % 2]
                    md = None
                    for fi in range(NCH):
                        w = ([("ve", P.n("ve"))] + bank_waits(R, bd)) if fi == 0 else []
                        md = P.mm(bd[0:128, 0:512], wdm(fi, fo), het[fi],
                                  fi == 0, fi == NCH - 1, waits=w, inc=("mm", 1))
                    i = fo % 2
                    vo = P.vcopy(orow[i], bd[0:128, 0:512],
                                 waits=[("mm", md)] + ([("dg", ost[i])] if ost[i] else []),
                                 inc=("ve", 1))
                    R.bank_set(bd, "ve", vo)
                    tgt = racc[fo * 128:(fo + 1) * 128, sl]
                    if e == 0 and half == 0:
                        ost[i] = P.gdma(tgt, orow[i], waits=[("ve", vo)])
                    else:
                        wacc = [("ve", vo)]
                        if first_acc:
                            wacc.append(("dg", prev_acc))
                        ost[i] = P.gdma_acc(tgt, orow[i], waits=wacc)
                    first_acc = False
            prev_acc = P.n("dg")
            R.wb.pop()
    P.barrier()
    R.xb.pop(); R.fb.pop(); R.rw.pop()
    residual_add_pass(P, R, xmid, racc, xout, add_bf16=False)
    P.barrier()




def emit_l1_attn_a(P, R, xin, xout, d):
    import math as _m
    h = P.dram_t("s_h", [D, T], BF16)
    xi = P.dram_t("s_xi", [4096, T], BF16)
    xact = P.dram_t("s_xact", [2048, T], BF16)
    qa = P.dram_t("s_qa", [2048, T], BF16)
    ka = P.dram_t("s_ka", [2048, T], BF16)
    va = P.dram_t("s_va", [2048, T], BF16)
    vtok = P.dram_t("s_vtok", [T, 2048], BF16)
    igfg = P.dram_t("s_igfg", [8, T], F32)
    dec = P.dram_t("s_dec", [8, T], F32)
    hT = P.dram_t("s_ht", [2048, T], BF16)
    hst = P.dram_t("s_hst", [2048, T], BF16)
    NH, DH = 4, 512
    scale = DH ** -0.5
    ln_pass(P, R, xin, h)
    P.barrier()
    R.xb.push()
    ev = mk_evac_store(P, R, xi)
    proj_pass(P, R, d["W_up"], h, nfi=8, nfo=32, evac=ev)
    P.barrier()
    R.xb.pop()
    R.xb.push(); R.rw.push(); R.fb.push()
    cw = [stage_colvec(P, R, d[f"conv_w{i}2d"], 16)[0] for i in range(4)]
    cbv, dvcb = stage_colvec(P, R, d["conv_b2d"], 16)
    ch = R.xb.alloc(T + 3)
    tb = R.fb.alloc(T)
    ob = R.xb.alloc(T)
    P.vmemset(ch[0:128, 0:3], 0.0, inc=("ve", 1))
    stv = 0
    for c in range(16):
        dv = P.sdma(ch[0:128, 3:T + 3], xi[c * 128:(c + 1) * 128, :],
                    waits=[("ve", P.n("ve")), ("ac", P.n("ac"))])
        w = [("ds", max(dv, stv)), ("dg", dvcb)]
        P.ts(tb, ch[0:128, 0:T], cw[0][0:128, c:c + 1], ALU.mult, waits=w)
        P.stt(tb, ch[0:128, 1:T + 1], cw[1][0:128, c:c + 1], ALU.mult, tb, ALU.add)
        P.stt(tb, ch[0:128, 2:T + 2], cw[2][0:128, c:c + 1], ALU.mult, tb, ALU.add)
        P.stt(tb, ch[0:128, 3:T + 3], cw[3][0:128, c:c + 1], ALU.mult, tb, ALU.add)
        vv = P.ts(tb, tb, cbv[0:128, c:c + 1], ALU.add, inc=("ve", 1))
        acv = P.act(ob, tb, AF.Silu, waits=[("ve", vv)], inc=("ac", 1))
        stv = P.sdma(xact[c * 128:(c + 1) * 128, :], ob, waits=[("ac", acv)])
    P.barrier()
    R.xb.pop(); R.rw.pop(); R.fb.pop()
    R.xb.push()
    skipz = lambda fi, fo: fi != fo
    ev = mk_evac_store(P, R, qa)
    proj_pass(P, R, d["Wq_bd"], xact, nfi=16, nfo=16, evac=ev, skip_zero=skipz)
    ev = mk_evac_store(P, R, ka)
    proj_pass(P, R, d["Wk_bd"], xact, nfi=16, nfo=16, evac=ev, skip_zero=skipz)
    ev = mk_evac_store(P, R, va)
    proj_pass(P, R, d["Wv_bd"], xi[0:2048, :], nfi=16, nfo=16, evac=ev,
              skip_zero=skipz)
    P.barrier()
    R.xb.pop()
    transpose_to_tok(P, R, va, vtok, nch=16)
    P.barrier()
    R.xb.push(); R.rw.push(); R.fb.push()
    bif, dvb = stage_colvec(P, R, d["b_igfg2d"], 1)
    igr = R.fb.alloc(QC, parts=8)
    wt, dgw = load_weight_lhsT(P, R, d["W_igfg"], 48, 1, fo_size=8)
    xt = [R.xb.alloc(QC) for _ in range(16)]
    Bq = [R.pb[0], R.pb[1], R.pb[2]]
    for qc in range(NQC):
        sl = slice(qc * QC, (qc + 1) * QC)
        B = Bq[qc % 3]
        mmv = None
        ci = 0
        for (src, base) in ((qa, 0), (ka, 16), (va, 32)):
            dv = 0
            for c in range(16):
                dv = P.sdma(xt[c], src[c * 128:(c + 1) * 128, sl],
                            waits=[("mm", P.n("mm"))] if c == 0 else [])
            for c in range(16):
                w = ([("ds", dv), ("dg", max(dgw, dvb))] + (bank_waits(R, B) if ci == 0 else []))
                mmv = P.mm(B[0:8, 0:512], wt(base + c, 0), xt[c], ci == 0,
                           ci == 47, waits=w, inc=("mm", 1))
                ci += 1
        acv = P.act(igr, B[0:8, 0:512], AF.Identity,
                    bias=bif[0:8, 0:1], waits=[("mm", mmv)], inc=("ac", 1))
        R.bank_set(B, "ac", acv)
        P.sdma(igfg[:, sl], igr, waits=[("ac", acv)])
    P.barrier()
    R.xb.pop(); R.rw.pop(); R.fb.pop()
    R.fb.push(); R.rw.push(); R.xb.push()
    CH = 512
    igt = R.fb.alloc(CH, parts=4)
    fgt = R.fb.alloc(CH, parts=4)
    lf = R.fb.alloc(CH, parts=4)
    sc1 = R.fb.alloc(CH, parts=4)
    cum = R.fb.alloc(CH, parts=4)
    gg = R.fb.alloc(CH, parts=4)
    MM = R.fb.alloc(CH, parts=4)
    carryC = R.fb.alloc(1, parts=4)
    carryM = R.fb.alloc(1, parts=4)
    mqc = R.fb.alloc(8, parts=4)
    gdr = P.dram_t("s_g4", [4, T], F32)
    P.vmemset(carryC[0:4, 0:1], 0.0)
    P.vmemset(carryM[0:4, 0:1], -1e30, inc=("ve", 1))
    stv = 0
    for j in range(T // CH):
        sl = slice(j * CH, (j + 1) * CH)
        dv = P.gdma(igt, igfg[0:4, sl], waits=[("ve", P.n("ve")), ("ac", P.n("ac"))])
        dv = P.gdma(fgt, igfg[4:8, sl])
        aca = P.act(sc1, fgt, AF.Abs, waits=[("dg", dv)])
        aca = P.act(sc1, sc1, AF.Exp, scale=-1.0)
        aca = P.act(sc1, sc1, AF.Ln, bias=1.0, inc=("ac", 1))
        P.ts(lf, fgt, 0.0, ALU.min, waits=[("ac", aca)])
        vv = P.tt(lf, lf, sc1, ALU.subtract, inc=("ve", 1))
        P.vscan(cum, lf, lf, carryC[0:4, 0:1], ALU.add, ALU.bypass,
                waits=[("ve", vv), ("ds", stv)])
        P.vcopy(carryC[0:4, 0:1], cum[0:4, CH - 1:CH])
        P.tt(gg, igt, cum, ALU.subtract)
        P.vscan(MM, gg, gg, carryM[0:4, 0:1], ALU.max, ALU.bypass)
        P.vcopy(carryM[0:4, 0:1], MM[0:4, CH - 1:CH])
        P.vred(mqc[0:4, j:j + 1], MM, ALU.max, inc=("ve", 1))
        P.sdma(dec[0:4, sl], cum, waits=[("ve", P.n("ve"))])
        P.sdma(dec[4:8, sl], MM)
        stv = P.sdma(gdr[0:4, sl], gg)
    mqc_d = P.dram_t("s_mqc", [4, 8], F32)
    dsm = P.sdma(mqc_d[:, :], mqc, waits=[("ve", P.n("ve"))])
    Gcols = R.fb.alloc(32 * 4)
    gkb = R.fb.alloc(128, parts=4)
    ech = R.rw.row_alloc(1)
    Bt = R.pb[3]
    for kb in range(32):
        dgk = P.sdma(gkb, gdr[0:4, kb * 128:(kb + 1) * 128],
                     waits=[("ve", P.n("ve"))])
        mmt = P.trans(Bt[0:128, 0:4], gkb, R.identf[0:4, 0:4],
                      waits=[("ds", dgk)] + bank_waits(R, Bt), inc=("mm", 1))
        P.vcopy(Gcols[0:128, kb * 4:(kb + 1) * 4], Bt[0:128, 0:4],
                waits=[("mm", mmt)])
        ve_ = P.vcopy(ech, Gcols[0:1, kb * 4:kb * 4 + 1], inc=("ve", 1))
        R.bank_set(Bt, "ve", ve_)
    mrow = R.rw.row_alloc(32)
    dmq = P.gdma(mrow[0:1, 0:32], mqc_d.rearrange("a b -> (a b)"),
                 waits=[("ve", P.n("ve")), ("ds", dsm)])
    mneg = R.fb.alloc(32)
    mmq = P.mm(R.pb[3][0:128, 0:32], R.onesrow_f, mrow[0:1, 0:32], True, True,
               waits=[("dg", dmq)] + bank_waits(R, R.pb[3]), inc=("mm", 1))
    vmq = P.ts(mneg, R.pb[3][0:128, 0:32], -1.0, ALU.mult, waits=[("mm", mmq)],
               inc=("ve", 1))
    R.bank_set(R.pb[3], "ve", vmq)
    cumrow = R.rw.row_alloc(T)
    efl = R.rw.row_alloc(QC)
    arow = R.rw.row_alloc(QC)
    rn_row = R.rw.row_alloc(QC)
    mtrow = R.rw.row_alloc(QC)
    alrow = R.rw.row_alloc(QC)
    rnb = R.xb.alloc(QC, parts=1)
    ptb_buf = R.xb.alloc(32 * QC)
    kst = [R.xb.alloc(QC), R.xb.alloc(QC)]
    qp = [R.xb.alloc(QC) for _ in range(4)]
    vst = [R.xb.alloc(256), R.xb.alloc(256)]
    EG = R.fb.alloc(32)
    mcol = R.fb.alloc(1)
    rb_sb = R.xb.alloc(QC)
    obt = [R.xb.alloc(QC), R.xb.alloc(QC)]
    masks = causal_mask_tiles(P, R)
    Bs = [R.pb[0], R.pb[1]]
    BL = R.pb[4]
    Bav = [R.pb[2], R.pb[5]]
    Bbc = R.pb[3]
    ost = [0, 0]
    for hh in range(NH):
        dcr = P.sdma(cumrow, dec[hh:hh + 1, :], waits=[("ve", P.n("ve"))])
        kT3 = ka[hh * 512:(hh + 1) * 512, :].rearrange("(c p) t -> p c t", p=128)
        vt3 = vtok.rearrange("(kb p) c -> p kb c", p=128)
        for qc in range(NQC):
            sl = slice(qc * QC, (qc + 1) * QC)
            kb_n = 4 * (qc + 1)
            mi = hh * 8 + qc
            aeg = P.act(EG, Gcols.rearrange("p (kb h) -> p h kb", h=4)[:, hh, :],
                        AF.Exp, bias=mneg[0:128, mi:mi + 1],
                        waits=[("ve", P.n("ve"))], inc=("ac", 1))
            veg = P.ts(EG, EG, float(scale), ALU.mult, waits=[("ac", aeg)],
                       inc=("ve", 1))
            acf = P.act(efl, cumrow[0:1, sl], AF.Exp, scale=-1.0,
                        bias=mneg[0:1, mi:mi + 1], waits=[("ve", veg), ("ds", dcr)],
                        inc=("ac", 1))
            dM = P.sdma(mtrow, dec[4 + hh:5 + hh, sl],
                        waits=[("ac", P.n("ac")), ("ve", P.n("ve"))])
            aal = P.act(alrow, mtrow, AF.Exp, bias=mneg[0:1, mi:mi + 1],
                        waits=[("ds", dM)], inc=("ac", 1))
            val = P.ts(alrow, alrow, 1e-6, ALU.mult, waits=[("ac", aal)],
                       inc=("ve", 1))
            dq = 0
            for c in range(4):
                dq = P.sdma(qp[c], qa[hh * 512 + c * 128:hh * 512 + (c + 1) * 128, sl],
                            waits=[("mm", P.n("mm"))] if c == 0 else [])
            mml = None
            for kb in range(kb_n):
                i = kb % 2
                dk = P.sdma(kst[i].rearrange("p (c t) -> p c t", t=128),
                            kT3[:, :, kb * 128:(kb + 1) * 128],
                            waits=[("mm", P.n("mm"))])
                bs = Bs[kb % 2]
                mms = None
                for c in range(4):
                    w = ([("ds", max(dk, dq))] + bank_waits(R, bs)) if c == 0 else []
                    mms = P.mm(bs[0:128, 0:512],
                               kst[i][0:128, c * 128:(c + 1) * 128], qp[c],
                               c == 0, c == 3, waits=w, inc=("mm", 1))
                pcol = ptb_buf[0:128, kb * QC:(kb + 1) * QC]
                vev = P.ts(pcol, bs[0:128, 0:512],
                           EG[0:128, kb:kb + 1], ALU.mult,
                           waits=[("mm", mms), ("ve", veg)], inc=("ve", 1))
                R.bank_set(bs, "ve", vev)
                dep = ("ve", vev)
                if kb >= kb_n - 4:
                    j = kb - (kb_n - 4)
                    vmk = P.tt(pcol, pcol, masks[j], ALU.mult, waits=[dep],
                               inc=("ve", 1))
                    dep = ("ve", vmk)
                mml = P.mm(BL[0:1, 0:512], R.onesb[0:128, 0:1], pcol,
                           kb == 0, kb == kb_n - 1,
                           waits=[dep] + (bank_waits(R, BL) if kb == 0 else []),
                           inc=("mm", 1))
            aab = P.act(arow, BL[0:1, 0:512], AF.Abs, waits=[("mm", mml)],
                        inc=("ac", 1))
            R.bank_set(BL, "ac", aab)
            P.tt(rn_row, arow, efl, ALU.max, waits=[("ac", max(aab, acf))])
            P.tt(rn_row, rn_row, alrow, ALU.add, waits=[("ve", val)])
            P.vrecip(rn_row, rn_row)
            vrn = P.vcopy(rnb, rn_row, inc=("ve", 1))
            if DBG_L1A:
                dEG = P.dram_t("s_dbg_eg", [32 * 128, 32], F32)
                dRN = P.dram_t("s_dbg_rn", [32 * 4, QC], F32)
                it = hh * 8 + qc
                P.sdma(dEG[it * 128:(it + 1) * 128, :], EG,
                       waits=[("ve", P.n("ve")), ("ac", P.n("ac"))])
                P.sdma(dRN[it * 4 + 0:it * 4 + 1, :], arow)
                P.sdma(dRN[it * 4 + 1:it * 4 + 2, :], efl)
                P.sdma(dRN[it * 4 + 2:it * 4 + 3, :], rn_row)
                P.sdma(dRN[it * 4 + 3:it * 4 + 4, 0:32], mneg[0:1, 0:32])
            mmc = P.mm(Bbc[0:128, 0:512], R.onesrow_b, rnb, True, True,
                       waits=[("ve", vrn)] + bank_waits(R, Bbc), inc=("mm", 1))
            vcb = P.vcopy(rb_sb, Bbc[0:128, 0:512], waits=[("mm", mmc)],
                          inc=("ve", 1))
            R.bank_set(Bbc, "ve", vcb)
            for half in range(2):
                mma = [None, None]
                for kb in range(kb_n):
                    i = kb % 2
                    dvv = P.sdma(vst[i],
                                 vt3[:, kb, hh * 512 + half * 256:
                                     hh * 512 + (half + 1) * 256],
                                 waits=[("mm", P.n("mm"))])
                    for ccl in range(2):
                        bank = Bav[ccl]
                        w = [("ds", dvv)]
                        if kb == 0:
                            w += bank_waits(R, bank)
                        mma[ccl] = P.mm(bank[0:128, 0:512],
                                        vst[i][0:128, ccl * 128:(ccl + 1) * 128],
                                        ptb_buf[0:128, kb * QC:(kb + 1) * QC],
                                        kb == 0, kb == kb_n - 1, waits=w,
                                        inc=("mm", 1))
                for ccl in range(2):
                    cc = half * 2 + ccl
                    bank = Bav[ccl]
                    i = cc % 2
                    vo = P.tt(obt[i], bank[0:128, 0:512], rb_sb, ALU.mult,
                              waits=[("mm", mma[ccl]), ("ve", vcb)] +
                              ([("ds", ost[i])] if ost[i] else []), inc=("ve", 1))
                    R.bank_set(bank, "ve", vo)
                    ost[i] = P.sdma(
                        hT[hh * 512 + cc * 128:hh * 512 + (cc + 1) * 128, sl],
                        obt[i], waits=[("ve", vo)])
    P.barrier()
    R.fb.pop(); R.rw.pop(); R.xb.pop()
    R.fb.push(); R.rw.push(); R.xb.push()
    nw, dv1 = stage_colvec(P, R, d["norm_w2d"], 16)
    sk, dv2 = stage_colvec(P, R, d["skip2d"], 16)
    hp = [R.xb.alloc(QC) for _ in range(4)]
    sqb = R.xb.alloc(QC)
    xab = R.xb.alloc(QC)
    zb = R.xb.alloc(QC)
    szb = R.xb.alloc(QC)
    tfb = R.fb.alloc(QC)
    obg = R.xb.alloc(QC)
    m_r = R.rw.row_alloc(QC)
    s_r = R.rw.row_alloc(QC)
    v_r = R.rw.row_alloc(QC)
    BS, BQ = R.pb[4], R.pb[5]
    BR, BM = R.pb[2], R.pb[3]
    stg = 0
    for qc in range(NQC):
        sl = slice(qc * QC, (qc + 1) * QC)
        for hh in range(NH):
            dv = 0
            for c in range(4):
                dv = P.sdma(hp[c], hT[hh * 512 + c * 128:hh * 512 + (c + 1) * 128, sl],
                            waits=[("mm", P.n("mm")), ("ac", P.n("ac")),
                                   ("ve", P.n("ve"))] if c == 0 else [])
            mmS = mmQ = None
            acq = 0
            for c in range(4):
                w = [("ds", dv)] + (bank_waits(R, BS) if c == 0 else [])
                mmS = P.mm(BS[0:1, :], R.onesb[0:128, 0:1], hp[c], c == 0, c == 3,
                           waits=w, inc=("mm", 1))
                acq = P.act(sqb, hp[c], AF.Square, waits=[("ds", dv), ("mm", mmQ or 0)],
                            inc=("ac", 1))
                mmQ = P.mm(BQ[0:1, :], R.onesb[0:128, 0:1], sqb, c == 0, c == 3,
                           waits=[("ac", acq)] + (bank_waits(R, BQ) if c == 0 else []),
                           inc=("mm", 1))
            P.ts(m_r, BS[0:1, :], 1.0 / DH, ALU.mult, waits=[("mm", mmS)])
            vv = P.ts(s_r, BQ[0:1, :], 1.0 / DH, ALU.mult, waits=[("mm", mmQ)],
                      inc=("ve", 1))
            R.bank_set(BS, "ve", vv)
            R.bank_set(BQ, "ve", vv)
            P.tt(v_r, m_r, m_r, ALU.mult)
            vv = P.tt(v_r, s_r, v_r, ALU.subtract, inc=("ve", 1))
            acv = P.act(s_r, v_r, AF.Ln, bias=1e-6, waits=[("ve", vv)])
            acv = P.act(s_r, s_r, AF.Exp, scale=-0.5, inc=("ac", 1))
            vv = P.tt(m_r, m_r, s_r, ALU.mult, waits=[("ac", acv)], inc=("ve", 1))
            mmR = P.mm(BR[0:128, 0:512], R.onesrow_f, s_r, True, True,
                       waits=[("ve", vv)] + bank_waits(R, BR), inc=("mm", 1))
            mmM = P.mm(BM[0:128, 0:512], R.onesrow_f, m_r, True, True,
                       waits=bank_waits(R, BM), inc=("mm", 1))
            velast = 0
            for c in range(4):
                cg = hh * 4 + c
                dza = P.sdma(xab, xact[cg * 128:(cg + 1) * 128, sl],
                             waits=[("ve", velast)] if velast else [])
                dza = P.sdma(zb, xi[2048 + cg * 128:2048 + (cg + 1) * 128, sl])
                acz = P.act(szb, zb, AF.Silu, waits=[("ds", dza), ("ve", P.n("ve"))],
                            inc=("ac", 1))
                P.tt(tfb, hp[c], BR[0:128, :], ALU.mult, waits=[("mm", mmM)])
                P.tt(tfb, tfb, BM[0:128, :], ALU.subtract)
                P.ts(tfb, tfb, nw[0:128, cg:cg + 1], ALU.mult)
                P.stt(tfb, xab, sk[0:128, cg:cg + 1], ALU.mult, tfb, ALU.add,
                      waits=[("dg", max(dv1, dv2)), ("ds", dza)])
                velast = P.tt(obg, tfb, szb, ALU.mult, waits=[("ac", acz),
                                                             ("ds", stg)],
                              inc=("ve", 1))
                stg = P.sdma(hst[cg * 128:(cg + 1) * 128, sl], obg,
                             waits=[("ve", velast)])
            R.bank_set(BR, "ve", velast)
            R.bank_set(BM, "ve", velast)
    P.barrier()
    R.fb.pop(); R.rw.pop(); R.xb.pop()
    R.fb.push()
    ev = mk_evac_residual(P, R, xin, xout)
    proj_pass(P, R, d["W_down"], hst, nfi=16, nfo=8, evac=ev)
    R.fb.pop()
    P.barrier()




def _l0b_rope_perm():
    perm = []
    for c in range(8):
        for hh in (2 * c, 2 * c + 1):
            perm += [hh * 64 + 2 * i for i in range(32)]
        for hh in (2 * c, 2 * c + 1):
            perm += [hh * 64 + 2 * i + 1 for i in range(32)]
    return np.array(perm)


def _bd4(W):
    out = np.zeros((2048, 2048), np.float32)
    for hh in range(512):
        out[hh * 4:(hh + 1) * 4, hh * 4:(hh + 1) * 4] = W[hh].T
    return out


def build_full():
    from contextlib import ExitStack
    import concourse.bass as bass
    nc = bass.Bass("TRN2", target_bir_lowering=False)
    P = Prog(nc)
    x0 = P.dram_t("x0", [D, T], F32, kind="ExternalInput")
    xs = [x0] + [P.dram_t(f"xc{i}", [D, T], F32) for i in range(1, 16)]
    xs.append(P.dram_t("out", [D, T], F32, kind="ExternalOutput"))

    def I(name, shape):
        return P.dram_t(name, list(shape), F32, kind="ExternalInput")

    d_l0a = {"W_qkv": I("l0a_W_qkv", [1024, 3072]),
             "b_qkv2d": I("l0a_b_qkv2d", [128, 24]),
             "W_vconv0": I("l0a_W_vconv0", [128, 8]),
             "W_vconv1": I("l0a_W_vconv1", [128, 8]),
             "W_vconv2": I("l0a_W_vconv2", [128, 8])}
    d_l0f = {"w_upT": I("l0f_w_upT", [1024, 4096]),
             "w_downT": I("l0f_w_downT", [4096, 1024])}
    d_l0b = {n: I("l0b_" + n, s) for n, s in
             [("w_down_qT", (1024, 512)), ("w_down_kvT", (1024, 512)),
              ("w_up_qTp", (512, 1024)), ("w_up_kTp", (512, 1024)),
              ("w_up_vT", (512, 1024)), ("w_outT", (1024, 1024))]}
    d_l0b["ln_scale"] = I("l0b_ln_scale", [1024])
    d_l0b["ln_bias"] = I("l0b_ln_bias", [1024])
    d_l1a = {"W_up": I("l1a_W_up", [1024, 4096]),
             "conv_b2d": I("l1a_conv_b2d", [128, 16]),
             "Wq_bd": I("l1a_Wq_bd", [2048, 2048]),
             "Wk_bd": I("l1a_Wk_bd", [2048, 2048]),
             "Wv_bd": I("l1a_Wv_bd", [2048, 2048]),
             "W_igfg": I("l1a_W_igfg", [6144, 8]),
             "b_igfg2d": I("l1a_b_igfg2d", [128, 1]),
             "norm_w2d": I("l1a_norm_w2d", [128, 16]),
             "skip2d": I("l1a_skip2d", [128, 16]),
             "W_down": I("l1a_W_down", [2048, 1024])}
    for i in range(4):
        d_l1a[f"conv_w{i}2d"] = I(f"l1a_conv_w{i}2d", [128, 16])
    d_l1f = {"W_rT": I("l1f_W_rT", [1024, 1024]),
             "W_kT": I("l1f_W_kT", [1024, 1024]),
             "W_vT": I("l1f_W_vT", [1024, 1024]),
             "mu_r2d": I("l1f_mu_r2d", [128, 8]),
             "mu_k2d": I("l1f_mu_k2d", [128, 8])}
    d_l2a = {"W_K": I("l2a_W_K", [16, 16]), "W_Q": I("l2a_W_Q", [16, 16]),
             "W_V": I("l2a_W_V", [16, 16]), "W_pred": I("l2a_W_pred", [16, 64, 64])}
    cn_dims = [(1024, 1024), (1024, 512), (512, 256), (256, 1)]
    gc_dims = [(1024, 512), (512, 256), (256, 1)]
    d_l2f = {}
    for i in range(4):
        d_l2f[f"cn_w{i+1}"] = I(f"l2f_cn_w{i+1}", list(cn_dims[i]))
        d_l2f[f"cn_b{i+1}_2d"] = I(f"l2f_cn_b{i+1}_2d", [128, max(1, cn_dims[i][1] // 128)])
    for i in range(3):
        d_l2f[f"gc_w{i+1}"] = I(f"l2f_gc_w{i+1}", list(gc_dims[i]))
        d_l2f[f"gc_b{i+1}_2d"] = I(f"l2f_gc_b{i+1}_2d", [128, max(1, gc_dims[i][1] // 128)])
    d_l2f["adaptive_range"] = I("l2f_adaptive_range", [1])
    d_l2b = {}
    for n, s in {"W_dkv": (1024, 128), "W_dq": (1024, 128), "W_uk": (128, 768),
                 "W_uv": (128, 1024), "W_uq": (128, 768), "W_qrp": (128, 256),
                 "W_krp": (1024, 256), "W_o": (1024, 1024)}.items():
        d_l2b[n] = I("l2b_" + n, list(s))
    for n, c in [("b_dkv2d", 1), ("b_dq2d", 1), ("b_uk2d", 6), ("b_uv2d", 8),
                 ("b_uq2d", 6), ("b_qr2d", 2), ("b_kr2d", 2), ("b_o2d", 8)]:
        d_l2b[n] = I("l2b_" + n, [128, c])
    d_l3a = {"W_qkv": I("l3a_W_qkv", [1024, 3072]),
             "b_qkv2d": I("l3a_b_qkv2d", [128, 24]),
             "W_o": I("l3a_W_o", [1024, 1024]),
             "b_o2d": I("l3a_b_o2d", [128, 8])}
    d_l3f = {"gate_wT": I("l3f_gate_wT", [1024, 8]),
             "shared_gateT": I("l3f_shared_gateT", [1024, 2048]),
             "shared_upT": I("l3f_shared_upT", [1024, 2048]),
             "shared_downT": I("l3f_shared_downT", [2048, 1024]),
             "r_gateT": [I(f"l3f_r_gateT{e}", [1024, 2048]) for e in range(8)],
             "r_upT": [I(f"l3f_r_upT{e}", [1024, 2048]) for e in range(8)],
             "r_downT": [I(f"l3f_r_downT{e}", [2048, 1024]) for e in range(8)]}
    d_l3b = {"WqkvuT": I("l3b_WqkvuT", [1024, 4096]),
             "bqkvu2d": I("l3b_bqkvu2d", [128, 32]),
             "WoutT": I("l3b_WoutT", [1024, 1024]),
             "bout2d": I("l3b_bout2d", [128, 8]),
             "theta": I("l3b_theta", [16, 1, 64]),
             "ln_w": I("l3b_ln_w", [1024]),
             "ln_b": I("l3b_ln_b", [1024])}

    es = ExitStack()
    E = es.enter_context
    R = Res(P, E)
    steps = [
        ("l0a_", lambda a, b: emit_l0_attn_a(P, R, a, b, d_l0a)),
        ("l0f_", lambda a, b: emit_l0_ffn(P, R, a, b, d_l0f, True)),
        ("l0b_", lambda a, b: emit_l0_attn_b(P, R, a, b, d_l0b)),
        ("l0f_", lambda a, b: emit_l0_ffn(P, R, a, b, d_l0f, False)),
        ("l1a_", lambda a, b: emit_l1_attn_a(P, R, a, b, d_l1a)),
        ("l1f_", lambda a, b: emit_l1_ffn(P, R, a, b, d_l1f)),
        ("l1b_", lambda a, b: emit_l1_attn_b(P, R, a, b)),
        ("l1f_", lambda a, b: emit_l1_ffn(P, R, a, b, d_l1f)),
        ("l2a_", lambda a, b: emit_l2_attn_a(P, R, a, b, d_l2a)),
        ("l2f_", lambda a, b: emit_l2_ffn(P, R, a, b, d_l2f, "t")),
        ("l2b_", lambda a, b: emit_l2_attn_b(P, R, a, b, d_l2b)),
        ("l2f_", lambda a, b: emit_l2_ffn(P, R, a, b, d_l2f, "t")),
        ("l3a_", lambda a, b: emit_l3_attn_a(P, R, a, b, d_l3a)),
        ("l3f_", lambda a, b: emit_l3_ffn(P, R, a, b, d_l3f, "t")),
        ("l3b_", lambda a, b: emit_l3_attn_b(P, R, a, b, d_l3b)),
        ("l3f_", lambda a, b: emit_l3_ffn(P, R, a, b, d_l3f, "t")),
    ]
    import os
    dbg = os.environ.get("MK_DBG_ARENA")
    for i, (pfx, f) in enumerate(steps):
        P.pfx = pfx
        f(xs[i], xs[i + 1])
        if dbg:
            print(f"step {i} {pfx}: fb={R.fb.off} xb={R.xb.off} rw={R.rw.off} "
                  f"wb={R.wb.off} cb={R.cb.off}", flush=True)
    nc2 = P.build()
    es.close()
    return nc2, P


def host_inputs(inputs):
    A = {k: np.ascontiguousarray(np.asarray(v, dtype=np.float32))
         for k, v in inputs.items()}
    A.pop("L3_attn_b_context", None)
    v2 = lambda v, n: np.ascontiguousarray(np.asarray(v).reshape(n, 128).T)
    TT = lambda w: np.ascontiguousarray(np.asarray(w).T)

    def v2d(v):
        v = np.asarray(v)
        if v.size < 128:
            out = np.zeros((128, 1), np.float32)
            out[:v.size, 0] = v
            return out
        return v2(v, v.size // 128)
    sh = {}
    sh["l0a_W_qkv"] = A["L0_attn_a_W_qkv"]
    sh["l0a_b_qkv2d"] = v2(A["L0_attn_a_b_qkv"], 24)
    for i in range(3):
        sh[f"l0a_W_vconv{i}"] = v2(A["L0_attn_a_W_vconv"][:, i], 8)
    sh["l0f_w_upT"] = TT(A["L0_ffn_w_up"])
    sh["l0f_w_downT"] = TT(A["L0_ffn_w_down"])
    perm = _l0b_rope_perm()
    sh["l0b_w_down_qT"] = TT(A["L0_attn_b_w_down_q"])
    sh["l0b_w_down_kvT"] = TT(A["L0_attn_b_w_down_kv"])
    sh["l0b_w_up_qTp"] = np.ascontiguousarray(TT(A["L0_attn_b_w_up_q"])[:, perm])
    sh["l0b_w_up_kTp"] = np.ascontiguousarray(TT(A["L0_attn_b_w_up_k"])[:, perm])
    sh["l0b_w_up_vT"] = TT(A["L0_attn_b_w_up_v"])
    sh["l0b_w_outT"] = TT(A["L0_attn_b_w_out"])
    sh["l0b_ln_scale"] = A["L0_attn_b_ln_scale"]
    sh["l0b_ln_bias"] = A["L0_attn_b_ln_bias"]
    sh["l1a_W_up"] = A["L1_attn_a_W_up"]
    for i in range(4):
        sh[f"l1a_conv_w{i}2d"] = v2(A["L1_attn_a_conv_w"][i], 16)
    sh["l1a_conv_b2d"] = v2(A["L1_attn_a_conv_b"], 16)
    sh["l1a_Wq_bd"] = _bd4(A["L1_attn_a_Wq"])
    sh["l1a_Wk_bd"] = _bd4(A["L1_attn_a_Wk"])
    sh["l1a_Wv_bd"] = _bd4(A["L1_attn_a_Wv"])
    sh["l1a_W_igfg"] = np.ascontiguousarray(
        np.concatenate([A["L1_attn_a_W_ig"], A["L1_attn_a_W_fg"]], 1))
    b8 = np.zeros((128, 1), np.float32)
    b8[:4, 0] = A["L1_attn_a_b_ig"]
    b8[4:8, 0] = A["L1_attn_a_b_fg"]
    sh["l1a_b_igfg2d"] = b8
    sh["l1a_norm_w2d"] = v2(A["L1_attn_a_norm_w"], 16)
    sh["l1a_skip2d"] = v2(A["L1_attn_a_skip"], 16)
    sh["l1a_W_down"] = A["L1_attn_a_W_down"]
    sh["l1f_W_rT"] = TT(A["L1_ffn_W_r"])
    sh["l1f_W_kT"] = TT(A["L1_ffn_W_k"])
    sh["l1f_W_vT"] = TT(A["L1_ffn_W_v"])
    sh["l1f_mu_r2d"] = v2(A["L1_ffn_mu_r"], 8)
    sh["l1f_mu_k2d"] = v2(A["L1_ffn_mu_k"], 8)
    sh["l2a_W_K"] = A["L2_attn_a_W_K"]
    sh["l2a_W_Q"] = A["L2_attn_a_W_Q"]
    sh["l2a_W_V"] = A["L2_attn_a_W_V"]
    sh["l2a_W_pred"] = A["L2_attn_a_W_pred"]
    for i in range(4):
        sh[f"l2f_cn_w{i+1}"] = A[f"L2_ffn_cn_w{i+1}"]
        sh[f"l2f_cn_b{i+1}_2d"] = v2d(A[f"L2_ffn_cn_b{i+1}"])
    for i in range(3):
        sh[f"l2f_gc_w{i+1}"] = A[f"L2_ffn_gc_w{i+1}"]
        sh[f"l2f_gc_b{i+1}_2d"] = v2d(A[f"L2_ffn_gc_b{i+1}"])
    sh["l2f_adaptive_range"] = A["L2_ffn_adaptive_range"]
    p2 = l2b_perm()
    sh["l2b_W_dkv"] = A["L2_attn_b_W_dkv"]
    sh["l2b_W_dq"] = A["L2_attn_b_W_dq"]
    sh["l2b_W_uk"] = A["L2_attn_b_W_uk"]
    sh["l2b_W_uv"] = A["L2_attn_b_W_uv"]
    sh["l2b_W_uq"] = A["L2_attn_b_W_uq"]
    sh["l2b_W_qrp"] = np.ascontiguousarray(A["L2_attn_b_W_qr"][:, p2])
    sh["l2b_W_krp"] = np.ascontiguousarray(A["L2_attn_b_W_kr"][:, p2])
    sh["l2b_W_o"] = A["L2_attn_b_W_o"]
    sh["l2b_b_dkv2d"] = A["L2_attn_b_b_dkv"].reshape(128, 1)
    sh["l2b_b_dq2d"] = A["L2_attn_b_b_dq"].reshape(128, 1)
    sh["l2b_b_uk2d"] = v2(A["L2_attn_b_b_uk"], 6)
    sh["l2b_b_uv2d"] = v2(A["L2_attn_b_b_uv"], 8)
    sh["l2b_b_uq2d"] = v2(A["L2_attn_b_b_uq"], 6)
    sh["l2b_b_qr2d"] = v2(A["L2_attn_b_b_qr"][p2], 2)
    sh["l2b_b_kr2d"] = v2(A["L2_attn_b_b_kr"][p2], 2)
    sh["l2b_b_o2d"] = v2(A["L2_attn_b_b_o"], 8)
    sh["l3a_W_qkv"] = A["L3_attn_a_W_qkv"]
    sh["l3a_b_qkv2d"] = v2(A["L3_attn_a_b_qkv"], 24)
    sh["l3a_W_o"] = A["L3_attn_a_W_o"]
    sh["l3a_b_o2d"] = v2(A["L3_attn_a_b_o"], 8)
    sh["l3f_gate_wT"] = TT(A["L3_ffn_gate_w"])
    sh["l3f_shared_gateT"] = TT(A["L3_ffn_shared_gate"])
    sh["l3f_shared_upT"] = TT(A["L3_ffn_shared_up"])
    sh["l3f_shared_downT"] = TT(A["L3_ffn_shared_down"])
    for e in range(8):
        sh[f"l3f_r_gateT{e}"] = TT(A["L3_ffn_r_gate"][e])
        sh[f"l3f_r_upT{e}"] = TT(A["L3_ffn_r_up"][e])
        sh[f"l3f_r_downT{e}"] = TT(A["L3_ffn_r_down"][e])
    sh["l3b_WqkvuT"] = TT(A["L3_attn_b_Wqkvu"])
    sh["l3b_bqkvu2d"] = v2(A["L3_attn_b_bqkvu"], 32)
    sh["l3b_WoutT"] = TT(A["L3_attn_b_Wout"])
    sh["l3b_bout2d"] = v2(A["L3_attn_b_bout"], 8)
    sh["l3b_theta"] = A["L3_attn_b_theta"]
    sh["l3b_ln_w"] = A["L3_attn_b_ln_w"]
    sh["l3b_ln_b"] = A["L3_attn_b_ln_b"]
    xcores = [np.ascontiguousarray(A["L0_attn_a_x"][b].T) for b in range(8)]
    return sh, xcores


def kernel(**inputs):
    from concourse.bass_utils import run_bass_kernel_spmd
    sh, xcores = host_inputs(inputs)
    nc2, P = build_full()
    in_maps = []
    for b in range(8):
        m = dict(sh)
        m["x0"] = xcores[b]
        m.update(P.const_arrays)
        in_maps.append(m)
    res = run_bass_kernel_spmd(nc2, in_maps, core_ids=list(range(8)))
    out = np.stack([np.asarray(res.results[b]["out"]).T for b in range(8)])
    return np.ascontiguousarray(out.astype(np.float32))
